# version 34 (final)
# baseline (speedup 1.0000x reference)
from collections import deque
from contextlib import ExitStack

import numpy as np

import concourse.bass as bass
import concourse.mybir as mybir
from concourse.bass_utils import run_bass_kernel_spmd

F32 = mybir.dt.float32
BF16 = mybir.dt.bfloat16
AF = mybir.ActivationFunctionType
ALU = mybir.AluOpType
AX = mybir.AxisListType

NCORES = 8
D = 2048
C = 16
TCORE = 1024
ST = 512
NS = 2
H = 16
NB = 4
HCN = 8
NEX = 9
EPS_LN = 1e-5
EPS_ATT = 1e-6
INV_D = 1.0 / D

FENCE_SE = True
FENCE_GE = True
POOL_ELU_ADD = True
POOL_A_REDUCE = False
POOL_LN_APPLY = True
POOL_LN_CAST = True
POOL_H_PROD = True
POOL_W2_EVAC = False


class Sem:
    __slots__ = ("h", "n")

    def __init__(self, h):
        self.h = h
        self.n = 0

    def inc(self, instr, k=1):
        instr.then_inc(self.h, k)
        self.n += k
        return self.n


class AutoSem(Sem):

    def inc(self, instr, k=1):
        return self.n


_AUTO_OPS = {
    "tensor_tensor", "tensor_scalar", "tensor_scalar_mul", "tensor_scalar_min",
    "tensor_scalar_max", "tensor_scalar_add", "scalar_tensor_tensor",
    "tensor_copy", "tensor_add", "tensor_sub", "tensor_mul", "memset",
    "reduce_max", "reduce_sum", "tensor_reduce", "reciprocal", "max",
    "activation", "copy", "square", "iota", "select",
}


class _Counting:

    def __init__(self, eng, sem):
        self._eng = eng
        self._sem = sem

    def __getattr__(self, name):
        attr = getattr(self._eng, name)
        if name in _AUTO_OPS and callable(attr):
            sem = self._sem

            def call(*a, **k):
                ins = attr(*a, **k)
                ins.then_inc(sem.h, 1)
                sem.n += 1
                return ins
            return call
        return attr


def _ap_ranges(args, kwargs):
    aps = []
    for x in args:
        if hasattr(x, "tensor") and hasattr(x, "offset"):
            aps.append(x)
    for x in kwargs.values():
        if hasattr(x, "tensor") and hasattr(x, "offset"):
            aps.append(x)

    def ext(ap):
        try:
            span = 1
            for step, count in ap.ap:
                span += abs(int(step)) * (int(count) - 1)
            off = int(ap.offset)
            return (ap.tensor.name, off, off + span)
        except Exception:
            return (ap.tensor.name, 0, 1 << 60)
    return ([ext(aps[0])] if aps else []), [ext(x) for x in aps[1:]]


def _overlap(r1, r2):
    for n1, s1, e1 in r1:
        for n2, s2, e2 in r2:
            if n1 == n2 and s1 < e2 and s2 < e1:
                return True
    return False


class AutoFenced:

    def __init__(self, eng, sem):
        self._eng = eng
        self._sem = sem
        self._recent = []
        self._covered = 0

    def __getattr__(self, name):
        attr = getattr(self._eng, name)
        if name in _AUTO_OPS and callable(attr):
            eng, sem = self._eng, self._sem

            def call(*a, **k):
                w, r = _ap_ranges(a, k)
                need = 0
                for val, pw, pr in self._recent:
                    if _overlap(r, pw) or _overlap(w, pw) or _overlap(w, pr):
                        need = max(need, val)
                if need > self._covered:
                    eng.wait_ge(sem.h, need)
                    self._covered = need
                    self._recent = [t for t in self._recent if t[0] > need]
                ins = attr(*a, **k)
                ins.then_inc(sem.h, 1)
                sem.n += 1
                self._recent.append((sem.n, w, r))
                if len(self._recent) > 24:
                    v = self._recent[0][0]
                    eng.wait_ge(sem.h, v)
                    self._covered = v
                    self._recent = [t for t in self._recent if t[0] > v]
                return ins
            return call
        return attr

    def wait_covered(self, val):
        if val > self._covered:
            self._covered = val
            self._recent = [t for t in self._recent if t[0] > val]


def build_program(debug=False):
    nc = bass.Bass("TRN2", target_bir_lowering=False)
    stack = ExitStack()
    nc._keepalive_stack = stack

    def sb(name, shape, dt):
        return stack.enter_context(nc.sbuf_tensor(name, shape, dt))

    xT = nc.dram_tensor("xT", [D, TCORE], F32, kind="ExternalInput").ap()
    wq_h = nc.dram_tensor("wq_h", [H, 128, D], F32, kind="ExternalInput").ap()
    wo_h = nc.dram_tensor("wo_h", [C, 128, D], F32, kind="ExternalInput").ap()
    mh_h = nc.dram_tensor("mh_h", [128, H, 516], F32, kind="ExternalInput").ap()
    w1_h = nc.dram_tensor("w1_h", [NEX, HCN, 128, D], F32, kind="ExternalInput").ap()
    w3_h = nc.dram_tensor("w3_h", [NEX, HCN, 128, D], F32, kind="ExternalInput").ap()
    w2_h = nc.dram_tensor("w2_h", [NEX, C, 128, 1024], F32, kind="ExternalInput").ap()
    gw_h = nc.dram_tensor("gw_h", [128, 128], F32, kind="ExternalInput").ap()
    bw_h = nc.dram_tensor("bw_h", [H, NB], F32, kind="ExternalInput").ap()
    nf_h = nc.dram_tensor("nf_h", [NB, 2048], F32, kind="ExternalInput").ap()
    out_h = nc.dram_tensor("out", [D, TCORE], F32, kind="ExternalOutput").ap()
    dbg = {}
    if debug:
        for nm, shape, dt in [
            ("dbg_x0", [128, C * TCORE], F32), ("dbg_xn", [128, C * ST], BF16),
            ("dbg_comb", [128, C * ST], BF16), ("dbg_attn", [128, C * ST], F32),
            ("dbg_x1", [128, C * ST], F32), ("dbg_y", [128, C * ST], BF16),
            ("dbg_scores", [8, ST], F32), ("dbg_sfm", [8, ST], F32),
            ("dbg_ffn", [128, C * ST], F32), ("dbg_wa", [H, NB], F32),
            ("dbg_waT", [NB, H], F32), ("dbg_sm", [H, NB], F32),
            ("dbg_s4", [NB, 4], F32), ("dbg_stm", [128, 32], F32),
            ("dbg_scl", [128, 32], F32), ("dbg_m01", [128, 8], F32),
        ]:
            dbg[nm] = nc.dram_tensor(nm, shape, dt, kind="ExternalOutput").ap()

    x_sb = sb("x_sb", [128, C * TCORE], F32)
    s1 = sb("s1", [128, C * ST], F32)
    act = sb("act", [128, C * ST], BF16)
    comb = sb("comb", [128, C * ST], BF16)
    NWB = 3
    wbig = [sb(f"wbig{i}", [128, 2 * D], BF16) for i in range(NWB)]
    w2b = [sb(f"w2b{i}", [128, 1024], BF16) for i in range(3)]
    mhb = [sb(f"mhb{i}", [128, 516], BF16) for i in range(6)]
    sig = [sb(f"sig{i}", [128, ST], BF16) for i in range(3)]
    scrF = sb("scrF", [128, 6 * ST], F32)

    def scrf(par, i):
        return scrF[:, (par * 3 + i) * ST:(par * 3 + i + 1) * ST]

    scrA = [sb(f"scrA{i}", [128, ST], F32) for i in range(2)]
    sgs = [sb(f"sgs{i}", [128, ST], BF16) for i in range(2)]
    sel4 = sb("sel4", [NB, NB * 128], BF16)
    c4b = [sb(f"c4b{i}", [NB, ST], BF16) for i in range(2)]
    scrN = sb("scrN", [NB, 2 * ST], F32)

    def scrNv(par):
        return scrN[:, par * ST:(par + 1) * ST]

    bc_a = sb("bc_a", [128, ST], F32)
    bc_b = sb("bc_b", [128, ST], F32)
    bc_s = [sb(f"bc_s{i}", [128, ST], F32) for i in range(2)]
    gw_sb = sb("gw_sb", [128, 128], BF16)
    bw_sb = sb("bw_sb", [H, NB], F32)
    sm_sb = sb("sm_sb", [H, NB], F32)
    wa_sb = sb("wa_sb", [H, NB], F32)
    waT_sb = sb("waT_sb", [NB, H], F32)
    small16 = sb("small16", [H, 8], F32)
    small4 = sb("small4", [NB, 4], F32)
    act_row = sb("act_row", [1, 4], F32)
    nf_sb = scrF[0:4, 0:2048]
    scores = sb("scores", [8, ST], F32)
    sfm = sb("sfm", [8, ST], F32)
    stm = sb("stm", [128, 32], F32)
    mask0 = sb("mask0", [128, 32], F32)
    stm2 = sb("stm2", [128, 32], F32)
    mask1 = sb("mask1", [128, 32], F32)
    scl = sb("scl", [128, 32], F32)
    m01 = sb("m01", [128, 8], F32)
    selrow = sb("selrow", [8, 128], F32)
    ident = sb("ident", [128, 128], F32)
    iota_r = sb("iota_r", [128, 128], F32)
    iota_c = sb("iota_c", [128, 1], F32)
    ones_col_bf = sb("ones_col_bf", [128, 1], BF16)
    ones_col_f = sb("ones_col_f", [128, 1], F32)
    ones_row_f = sb("ones_row_f", [1, 128], F32)
    consts = sb("consts", [128, 8], F32)

    pb = [stack.enter_context(nc.psum_tensor(f"pb{i}", [128, 512], F32)) for i in range(8)]

    qx = Sem(stack.enter_context(nc.semaphore("qx")))
    qs = Sem(stack.enter_context(nc.semaphore("qs")))
    qg = Sem(stack.enter_context(nc.semaphore("qg")))
    qwp = [Sem(stack.enter_context(nc.semaphore(f"qw{i}"))) for i in range(16)]
    qo = Sem(stack.enter_context(nc.semaphore("qo")))
    pe = Sem(stack.enter_context(nc.semaphore("pe")))
    ve = AutoSem(stack.enter_context(nc.semaphore("ve")))
    ac = AutoSem(stack.enter_context(nc.semaphore("ac")))
    gp = AutoSem(stack.enter_context(nc.semaphore("gp")))

    TE = nc.tensor
    VE = AutoFenced(nc.vector, ve)
    SE = AutoFenced(nc.scalar, ac) if FENCE_SE else _Counting(nc.scalar, ac)
    GE = AutoFenced(nc.gpsimd, gp) if FENCE_GE else _Counting(nc.gpsimd, gp)
    SY = nc.sync

    def join(*engs):
        for e in engs:
            for s in (pe, ve, ac, gp):
                e.wait_ge(s.h, s.n)

    def ve_sync(instr):
        return ve.n

    def xc(s, c):
        return x_sb[:, c * TCORE + s * ST: c * TCORE + s * ST + ST]

    def actc(c):
        return act[:, c * ST:(c + 1) * ST]

    def combc(c):
        return comb[:, c * ST:(c + 1) * ST]

    def s1c(c):
        return s1[:, c * ST:(c + 1) * ST]

    r_su = scrA[0][0:1, :]
    r_msq = scrA[0][0:1, :]
    r_sq = scrA[1][0:1, :]
    r_mean = scrN[0:1, 0:ST]
    r_var = scrN[0:1, ST:2 * ST]
    r_rstd = bc_s[0][0:1, :]
    r_b = sfm[0:1, :]

    res = {}

    def wait_res(eng, key):
        for s, v in res.get(key, ()):
            eng.wait_ge(s.h, v)

    def set_res(key, *pairs):
        res[key] = list(pairs)

    NBUF = {"wbig": NWB, "mhb": 6, "w2b": 3}
    wreg = {k: 0 for k in NBUF}
    wcon = {k: 0 for k in NBUF}
    free_events = {k: [] for k in NBUF}

    def reg_slot(cls):
        s_ = wreg[cls] % NBUF[cls]
        wreg[cls] += 1
        return s_

    def con_slot(cls):
        s_ = wcon[cls] % NBUF[cls]
        wcon[cls] += 1
        return s_

    def free_slot(cls, sem, val):
        free_events[cls].append((sem, val))

    def slot_wait(cls, seq):
        if seq >= NBUF[cls]:
            sem, val = free_events[cls][seq - NBUF[cls]]
            GE.wait_ge(sem.h, val)

    def slot_ready(cls, seq):
        return seq < NBUF[cls] or len(free_events[cls]) > seq - NBUF[cls]

    units = []
    classQ = {c: [] for c in NBUF}
    cls_issued = {c: 0 for c in NBUF}
    cls_consumed = {c: 0 for c in NBUF}
    consumed = [0]
    issue_seq = [0]

    def q_register(cls, ready_fn, issue_fn):
        units.append((cls, len(classQ[cls])))
        classQ[cls].append([ready_fn, issue_fn, None])

    def q_pump(k):
        for cls in NBUF:
            q = classQ[cls]
            while (cls_issued[cls] < len(q)
                   and cls_issued[cls] < cls_consumed[cls] + k
                   and q[cls_issued[cls]][0]()):
                rec = q[cls_issued[cls]]
                sem = qwp[issue_seq[0] % 16]
                issue_seq[0] += 1
                GE.wait_ge(sem.h, sem.n)
                rec[1](sem)
                rec[2] = (sem, sem.n)
                cls_issued[cls] += 1

    def q_wait(eng):
        cls, seq = units[consumed[0]]
        consumed[0] += 1
        cls_consumed[cls] += 1
        while classQ[cls][seq][2] is None:
            q_pump(3)
        sem, v = classQ[cls][seq][2]
        eng.wait_ge(sem.h, v)

    def mk_head_unit(h):
        wseq = wreg["wbig"]
        mseq = wreg["mhb"]
        ws = reg_slot("wbig")
        ms = reg_slot("mhb")

        def ready():
            return slot_ready("wbig", wseq) and slot_ready("mhb", mseq)

        def fn(sem):
            slot_wait("wbig", wseq)
            sem.inc(GE.dma_start(out=wbig[ws][:, 0:D], in_=wq_h[h]), 16)
            slot_wait("mhb", mseq)
            sem.inc(GE.dma_start(out=mhb[ms][:], in_=mh_h[:, h, :]), 16)
        return "wbig", ready, fn

    def mk_wo_unit(fc):
        wseq = wreg["wbig"]
        ws = reg_slot("wbig")

        def ready():
            return slot_ready("wbig", wseq)

        def fn(sem):
            slot_wait("wbig", wseq)
            sem.inc(GE.dma_start(out=wbig[ws][:, 0:D], in_=wo_h[fc]), 16)
        return "wbig", ready, fn

    def mk_w13_unit(e, hc, g):
        wseq = wreg["wbig"]
        ws = reg_slot("wbig")

        def ready():
            return slot_ready("wbig", wseq)

        def fn(sem):
            slot_wait("wbig", wseq)
            sem.inc(GE.dma_start(out=wbig[ws][:, 0:D], in_=w1_h[e, hc]), 16)
            sem.inc(GE.dma_start(out=wbig[ws][:, D:2 * D], in_=w3_h[e, hc]), 16)
        return "wbig", ready, fn

    def mk_w2_unit(e, fc, g):
        w2seq = wreg["w2b"]
        w2s = reg_slot("w2b")

        def ready():
            return slot_ready("w2b", w2seq)

        def fn(sem):
            slot_wait("w2b", w2seq)
            sem.inc(GE.dma_start(out=w2b[w2s][:], in_=w2_h[e, fc]), 16)
        return "w2b", ready, fn

    xT_v = xT.rearrange("(c p) t -> p c t", p=128)
    x_sb_v = x_sb[:].rearrange("p (c t) -> p c t", c=C)
    for g in range(4):
        qx.inc(SY.dma_start(out=x_sb_v[:, 4 * g:4 * g + 4, :],
                            in_=xT_v[:, 4 * g:4 * g + 4, :]), 16)
    v_x = qx.n
    qs.inc(SY.dma_start(out=bw_sb[:], in_=bw_h), 16)
    qs.inc(SY.dma_start(out=nf_sb[:], in_=nf_h), 16)
    qg.inc(GE.dma_start(out=gw_sb[:], in_=gw_h), 16)
    v_small = qs.n

    GE.iota(iota_r[:], pattern=[[1, 128]], base=0, channel_multiplier=0,
            allow_small_or_imprecise_dtypes=True)
    v_iota = gp.inc(GE.iota(iota_c[:], pattern=[[0, 1]], base=0, channel_multiplier=1,
                            allow_small_or_imprecise_dtypes=True))

    VE.memset(ones_col_bf[:], 1.0)
    VE.memset(ones_col_f[:], 1.0)
    VE.memset(ones_row_f[:], 1.0)
    VE.memset(consts[:, 0:1], EPS_LN)
    VE.memset(consts[:, 1:2], -0.5)
    VE.memset(consts[:, 2:3], -1.0)
    VE.wait_ge(gp.h, v_iota)
    VE.tensor_scalar(ident[:], iota_r[:], iota_c[:, 0:1], None, op0=ALU.is_equal)
    for n in range(NB):
        VE.tensor_scalar(sel4[:, n * 128:(n + 1) * 128],
                         iota_c[0:NB, 0:1].to_broadcast([NB, 128]), float(n), None,
                         op0=ALU.is_equal)

    VE.wait_ge(qs.h, v_small)
    ve_sync(VE.reduce_sum(small4[:, 0:1], nf_sb[:], axis=AX.X))
    VE.tensor_scalar(small4[:, 1:2], small4[:, 0:1], EPS_ATT, None, op0=ALU.is_ge)
    VE.wait_ge(qs.h, v_small)
    ve_sync(VE.reduce_max(small16[:, 0:1], bw_sb[:], axis=AX.X))
    v2 = ve.inc(VE.tensor_tensor(sm_sb[:], bw_sb[:],
                                 small16[:, 0:1].to_broadcast([H, NB]), op=ALU.subtract))
    SE.wait_ge(ve.h, v2)
    v3 = ac.inc(SE.activation(sm_sb[:], sm_sb[:], AF.Exp))
    VE.wait_ge(ac.h, v3)
    ve_sync(VE.reduce_sum(small16[:, 1:2], sm_sb[:], axis=AX.X))
    ve_sync(VE.reciprocal(small16[:, 2:3], small16[:, 1:2]))
    v4 = ve.inc(VE.tensor_tensor(sm_sb[:], sm_sb[:],
                                 small16[:, 2:3].to_broadcast([H, NB]), op=ALU.mult))
    TE.wait_ge(ve.h, v4)
    v5 = pe.inc(TE.transpose(pb[6][0:1, 0:4], small4[0:4, 1:2], ident[0:4, 0:4]))
    SE.wait_ge(pe.h, v5)
    v6 = ac.inc(SE.copy(act_row[:], pb[6][0:1, 0:4]))
    TE.wait_ge(ac.h, v6)
    v7 = pe.inc(TE.matmul(pb[7][0:16, 0:4], ones_row_f[0:1, 0:16], act_row[:],
                          start=True, stop=True))
    VE.wait_ge(pe.h, v7)
    v8 = ve.inc(VE.tensor_tensor(wa_sb[:], pb[7][0:16, 0:4], sm_sb[:], op=ALU.mult))
    TE.wait_ge(ve.h, v8)
    v9 = pe.inc(TE.transpose(pb[6][0:4, 0:16], wa_sb[:], ident[0:16, 0:16]))
    SE.wait_ge(pe.h, v9)
    ac.inc(SE.copy(waT_sb[:], pb[6][0:4, 0:16]))

    def ln_stat_chunk(c, src_ap, ready=None):
        if c == 0:
            TE.wait_ge(qx.h, v_x)
            wait_res(TE, "pbn0_rd")
            wait_res(TE, "pbn1_rd")
        if ready is not None:
            TE.wait_ge(ready[0].h, ready[1])
            SE.wait_ge(ready[0].h, ready[1])
        TE.matmul(pb[2][0:1, :], ones_col_f[:], src_ap,
                  start=(c == 0), stop=(c == C - 1))
        v_sq = ac.inc(SE.activation(actc(c), src_ap, AF.Square))
        TE.wait_ge(ac.h, v_sq)
        ins = TE.matmul(pb[3][0:1, :], ones_col_bf[:], actc(c),
                        start=(c == 0), stop=(c == C - 1))
        return pe.inc(ins) if c == C - 1 else None

    def emit_ln_stats(src_fn, src_ready=None):
        v_stats = None
        for c in range(C):
            r = src_ready[c] if src_ready is not None else None
            v = ln_stat_chunk(c, src_fn(c), ready=r)
            if v is not None:
                v_stats = v
        return v_stats

    def emit_ln_finish(s, src_fn, mode, v_stats, bct=None, defer=False):
        bca_t, bcb_t = bct if bct is not None else (bc_a, bc_b)
        SE.wait_ge(pe.h, v_stats)
        SE.copy(r_su, pb[2][0:1, :])
        v_ev = ac.inc(SE.copy(r_sq, pb[3][0:1, :]))
        TE.wait_ge(ac.h, v_ev)
        TE.matmul(pb[2][0:1, :], sel4[0:4, 0:1], sel4[0:4, :], start=True, stop=True)
        set_res("pbn0_rd", (ac, v_ev))
        set_res("pbn1_rd", (ac, v_ev))
        VE.wait_ge(ac.h, v_ev)
        ve_sync(VE.tensor_scalar_mul(r_mean, r_su, INV_D))
        VE.tensor_tensor(r_msq, r_mean, r_mean, op=ALU.mult)
        v_var = ve.inc(VE.scalar_tensor_tensor(r_var, r_sq, INV_D, r_msq,
                                               op0=ALU.mult, op1=ALU.subtract))
        TE.wait_ge(ve.h, v_var)
        TE.matmul(pb[2][0:1, :], sel4[0:4, 0:1], sel4[0:4, :], start=True, stop=True)
        SE.wait_ge(ve.h, v_var)
        SE.activation(r_var, r_var, AF.Ln, bias=consts[0:1, 0:1])
        v_rstd = ac.inc(SE.activation(r_rstd, r_var, AF.Exp, scale=consts[0:1, 1:2]))
        TE.wait_ge(ac.h, v_rstd)
        TE.matmul(pb[2][0:1, :], sel4[0:4, 0:1], sel4[0:4, :], start=True, stop=True)
        VE.wait_ge(ac.h, v_rstd)
        v_b = ve.inc(VE.scalar_tensor_tensor(r_b, r_mean, -1.0, r_rstd,
                                             op0=ALU.mult, op1=ALU.mult))
        TE.wait_ge(ve.h, v_b)
        for key in ("pbbc_rd", "pbcb0_rd", "pbcb1_rd", "pb4g_rd", "pbw20_rd"):
            wait_res(TE, key)
        TE.matmul(pb[4][:, :], ones_row_f[:], r_rstd, start=True, stop=True)
        v_bc = pe.inc(TE.matmul(pb[5][:, :], ones_row_f[:], r_b, start=True, stop=True))
        SE.wait_ge(pe.h, v_bc)
        if bct is not None:
            wait_res(SE, "bcs_rd0")
            wait_res(SE, "bcs_rd1")
        SE.copy(bca_t[:], pb[4][:, :])
        v_bcev = ac.inc(SE.copy(bcb_t[:], pb[5][:, :]))
        set_res("pbbc_rd", (ac, v_bcev))
        if defer:
            return v_bcev
        VE.wait_ge(ac.h, v_bcev)
        applies = []
        for c in range(C):
            VE.tensor_tensor(scrf(0, 0), src_fn(c), bca_t[:], op=ALU.mult)
            if mode == "pre":
                ins = VE.tensor_tensor(actc(c), scrf(0, 0), bcb_t[:], op=ALU.add)
            else:
                VE.tensor_tensor(scrf(0, 1), scrf(0, 0), bcb_t[:], op=ALU.add)
                ins = VE.tensor_tensor(xc(s, c), scrf(0, 1), xc(s, c), op=ALU.add)
            applies.append((ve, ve.inc(ins)))
        if bct is not None:
            set_res("bcs_rd0", (ve, applies[-1][1]))
            set_res("bcs_rd1", (ve, applies[-1][1]))
        return applies

    def emit_ln_applies_deferred(s, v_bcev):
        wait_res(VE, "scrf00_rd")
        wait_res(VE, "scrf012_rd")
        VE.wait_ge(ac.h, v_bcev)
        aps = []
        for c in range(C):
            VE.tensor_tensor(scrf(0, 0), s1c(c), bc_a[:], op=ALU.mult)
            VE.tensor_tensor(scrf(0, 1), scrf(0, 0), bc_b[:], op=ALU.add)
            aps.append(ve.inc(VE.tensor_tensor(xc(s, c), scrf(0, 1), xc(s, c),
                                               op=ALU.add)))
        set_res("pbbc_rd", (ve, aps[-1]))
        set_res("scrf00_rd", (ve, aps[-1]))
        set_res("scrf012_rd", (ve, aps[-1]))
        return aps

    def emit_attn(s, q0_ready):
        qstate = {}
        estate = {}
        nstate = {}
        last_cev = [0]

        def emit_q(h):
            par = h % 2
            ws = con_slot("wbig")
            ms = con_slot("mhb")
            q_wait(TE)
            wait_res(TE, f"pbq{par}_rd")
            ins = None
            for kc in range(C):
                if h == 0:
                    TE.wait_ge(q0_ready[kc][0].h, q0_ready[kc][1])
                ins = TE.matmul(pb[par][:, :], wbig[ws][:, kc * 128:(kc + 1) * 128],
                                actc(kc), start=(kc == 0), stop=(kc == C - 1))
            v_q = pe.inc(ins)
            free_slot("wbig", pe, v_q)
            qstate[h] = (v_q, ms)

        def emit_elu(h):
            par = h % 2
            sg = h % 3
            v_q, ms = qstate.pop(h)
            VE.wait_ge(pe.h, v_q)
            wait_res(VE, f"scrf{par}0_rd")
            v_min = ve.inc(VE.tensor_scalar_min(scrf(par, 0), pb[par][:, :], 0.0))
            SE.wait_ge(ve.h, v_min)
            wait_res(SE, f"scrf{par}12_rd")
            v_exp = ac.inc(SE.activation(scrf(par, 1), scrf(par, 0), AF.Exp))
            v_relu = ac.inc(SE.activation(scrf(par, 2), pb[par][:, :], AF.Relu))
            set_res(f"pbq{par}_rd", (ve, v_min), (ac, v_relu))
            set_res(f"scrf{par}0_rd", (ac, v_exp))
            EADD = GE if POOL_ELU_ADD else VE
            sem_eadd = gp if POOL_ELU_ADD else ve
            EADD.wait_ge(ac.h, v_relu)
            wait_res(EADD, f"sig{sg}_rd")
            v_sig = sem_eadd.inc(EADD.tensor_tensor(sig[sg][:], scrf(par, 1),
                                                    scrf(par, 2), op=ALU.add))
            set_res(f"scrf{par}12_rd", (sem_eadd, v_sig))
            estate[h] = (ms, (sem_eadd, v_sig))

        def emit_nc(h):
            par = h % 2
            sg = h % 3
            ms, (sem_sig, v_sig) = estate.pop(h)
            TE.wait_ge(sem_sig.h, v_sig)
            wait_res(TE, f"pbn{par}_rd")
            v_n = pe.inc(TE.matmul(pb[2 + par][0:4, :], mhb[ms][:, 512:516],
                                   sig[sg][:], start=True, stop=True))
            VE.wait_ge(pe.h, v_n)
            wait_res(VE, f"scrN{par}_rd")
            v_nmax = ve.inc(VE.tensor_scalar_max(scrNv(par), pb[2 + par][0:4, :], EPS_ATT))
            SE.wait_ge(ve.h, v_nmax)
            SE.activation(scrNv(par), scrNv(par), AF.Ln)
            v_cexp = ac.inc(SE.activation(scrNv(par), scrNv(par), AF.Exp,
                                          scale=consts[0:NB, 2:3]))
            set_res(f"scrN{par}_rd", (ac, v_cexp))
            VE.wait_ge(ac.h, v_cexp)
            wait_res(VE, f"c4_{par}_rd")
            v_c4 = ve.inc(VE.tensor_tensor(c4b[par][:], scrNv(par),
                                           waT_sb[:, h:h + 1].to_broadcast([NB, ST]),
                                           op=ALU.mult))
            nstate[h] = (ms, v_c4, v_nmax, (sem_sig, v_sig))

        def emit_fin(h):
            par = h % 2
            sg = h % 3
            ms, v_c4, v_nmax, (sem_sig, v_sig) = nstate.pop(h)
            banks = [2 + par, 4, 5, 2 + par]
            v_cb = [None] * NB
            TE.wait_ge(ve.h, v_c4)
            wait_res(TE, "pbbc_rd")
            for n in range(3):
                if n > 0:
                    wait_res(TE, f"pbcb{n - 1}_rd")
                v_cb[n] = pe.inc(TE.matmul(pb[banks[n]][:, :],
                                           sel4[:, n * 128:(n + 1) * 128],
                                           c4b[par][:], start=True, stop=True))
            v_sg = [None] * NB
            v_A = None
            for n in range(NB):
                VE.wait_ge(pe.h, v_cb[n])
                if n == 0:
                    VE.wait_ge(sem_sig.h, v_sig)
                wait_res(VE, f"sgs{n % 2}_rd")
                v_sg[n] = ve.inc(VE.tensor_tensor(sgs[n % 2][:], sig[sg][:],
                                                  pb[banks[n]][:, :], op=ALU.mult))
                if n == 0:
                    TE.wait_ge(ve.h, v_sg[0])
                    v_cb[3] = pe.inc(TE.matmul(pb[banks[3]][:, :],
                                               sel4[:, 3 * 128:4 * 128],
                                               c4b[par][:], start=True, stop=True))
                TE.wait_ge(ve.h, v_sg[n])
                if n == 0:
                    wait_res(TE, f"pbA{par}_rd")
                v_A = pe.inc(TE.matmul(pb[6 + par][:, :], mhb[ms][:, n * 128:(n + 1) * 128],
                                       sgs[n % 2][:], start=(n == 0), stop=(n == NB - 1)))
                set_res(f"sgs{n % 2}_rd", (pe, v_A))
            free_slot("mhb", pe, v_A)
            set_res(f"sig{sg}_rd", (ve, v_sg[3]))
            set_res(f"c4_{par}_rd", (pe, v_cb[3]))
            set_res(f"pbn{par}_rd", (ve, v_sg[3]))
            set_res("pbcb0_rd", (ve, v_sg[1]))
            set_res("pbcb1_rd", (ve, v_sg[2]))
            q_pump(6)
            SE.wait_ge(pe.h, v_A)
            v_cev = ac.inc(SE.copy(comb[:, h * 512:(h + 1) * 512], pb[6 + par][:, :]))
            set_res(f"pbA{par}_rd", (ac, v_cev))
            last_cev[0] = v_cev

        for h in range(H):
            q_pump(6)
            emit_q(h)
            if h >= 1:
                emit_elu(h - 1)
            if h >= 2:
                emit_nc(h - 2)
            if h >= 3:
                emit_fin(h - 3)
        emit_elu(H - 1)
        emit_nc(H - 2)
        emit_fin(H - 3)
        emit_nc(H - 1)
        emit_fin(H - 2)
        emit_fin(H - 1)
        return last_cev[0]

    def emit_wo(s, v_comb, s1_guard=None):
        v_stats = None
        evs = {}
        for fc in range(C):
            par = fc % 2
            ws = con_slot("wbig")
            q_wait(TE)
            wait_res(TE, f"pbwo{par}_rd")
            wait_res(TE, f"pbq{par}_rd")
            if fc == 0:
                TE.wait_ge(ac.h, v_comb)
            ins = None
            for hc in range(C):
                ins = TE.matmul(pb[par][:, :], wbig[ws][:, hc * 128:(hc + 1) * 128],
                                combc(hc), start=(hc == 0), stop=(hc == C - 1))
            v_wo = pe.inc(ins)
            free_slot("wbig", pe, v_wo)
            q_pump(6)
            SE.wait_ge(pe.h, v_wo)
            if s1_guard is not None:
                SE.wait_ge(ve.h, s1_guard[fc])
            v_ev = ac.inc(SE.copy(s1c(fc), pb[par][:, :]))
            set_res(f"pbwo{par}_rd", (ac, v_ev))
            evs[fc] = (ac, v_ev)
            if fc >= 2:
                ln_stat_chunk(fc - 2, s1c(fc - 2), ready=evs[fc - 2])
        for fc in (C - 2, C - 1):
            v = ln_stat_chunk(fc, s1c(fc), ready=evs[fc])
            if v is not None:
                v_stats = v
        return v_stats

    def emit_gate(s, ready):
        TE.wait_ge(qg.h, qg.n)
        wait_res(TE, "pbwo0_rd")
        wait_res(TE, "pbwo1_rd")
        ins = None
        for kc in range(C):
            TE.wait_ge(ready[kc][0].h, ready[kc][1])
            ins = TE.matmul(pb[0][0:8, :], gw_sb[:, kc * 8:(kc + 1) * 8], actc(kc),
                            start=(kc == 0), stop=(kc == C - 1))
        v_g = pe.inc(ins)
        SE.wait_ge(pe.h, v_g)
        v_sco = ac.inc(SE.activation(scores[:], pb[0][0:8, :], AF.Sigmoid))
        TE.wait_ge(ac.h, v_sco)
        ins = None
        for tt in range(4):
            ins = TE.transpose(pb[1][:, tt * 8:(tt + 1) * 8],
                               scores[:, tt * 128:(tt + 1) * 128], ident[0:8, 0:8])
        v_t = pe.inc(ins)
        SE.wait_ge(pe.h, v_t)
        v_stm = ac.inc(SE.copy(stm[:], pb[1][:, 0:32]))
        VE.wait_ge(ac.h, v_stm)
        stm3 = stm[:].rearrange("p (a b) -> p a b", a=4)
        ve_sync(VE.reduce_max(m01[:, 0:4].unsqueeze(2), stm3, axis=AX.X))
        m0b = m01[:, 0:4].unsqueeze(2).to_broadcast([128, 4, 8])
        VE.tensor_tensor(mask0[:].rearrange("p (a b) -> p a b", a=4), stm3, m0b,
                         op=ALU.is_equal)
        VE.scalar_tensor_tensor(stm2[:], mask0[:], -1e30, stm[:],
                                op0=ALU.mult, op1=ALU.add)
        stm23 = stm2[:].rearrange("p (a b) -> p a b", a=4)
        ve_sync(VE.reduce_max(m01[:, 4:8].unsqueeze(2), stm23, axis=AX.X))
        m1b = m01[:, 4:8].unsqueeze(2).to_broadcast([128, 4, 8])
        VE.tensor_tensor(mask1[:].rearrange("p (a b) -> p a b", a=4), stm23, m1b,
                         op=ALU.is_equal)
        VE.tensor_tensor(scl[:].rearrange("p (a b) -> p a b", a=4),
                         mask0[:].rearrange("p (a b) -> p a b", a=4), m0b, op=ALU.mult)
        VE.tensor_tensor(mask1[:].rearrange("p (a b) -> p a b", a=4),
                         mask1[:].rearrange("p (a b) -> p a b", a=4), m1b, op=ALU.mult)
        v_scl = ve.inc(VE.tensor_tensor(scl[:], scl[:], mask1[:], op=ALU.add))
        TE.wait_ge(ve.h, v_scl)
        wait_res(TE, "pbn0_rd")
        ins = None
        for tt in range(4):
            ins = TE.transpose(pb[2][0:8, tt * 128:(tt + 1) * 128],
                               scl[:, tt * 8:(tt + 1) * 8], ident[:, :])
        v_bt = pe.inc(ins)
        SE.wait_ge(pe.h, v_bt)
        v_sfm = ac.inc(SE.copy(sfm[:], pb[2][0:8, :]))
        set_res("pbn0_rd", (ac, v_sfm))
        return v_sfm

    def emit_ffn(s, v_sfm):
        TE.wait_ge(ac.h, v_sfm)
        v_stats = None
        v_bse = {}

        def emit_sbcast(e):
            bp = e % 2
            wait_res(VE, "selrow_rd")
            v_sel = ve.inc(VE.tensor_scalar(
                selrow[:], iota_c[0:8, 0:1].to_broadcast([8, 128]), float(e),
                None, op0=ALU.is_equal))
            TE.wait_ge(ve.h, v_sel)
            wait_res(TE, "pb4g_rd")
            v_sb = pe.inc(TE.matmul(pb[4][:, :], selrow[:],
                                    sfm[:], start=True, stop=True))
            set_res("selrow_rd", (pe, v_sb))
            SE.wait_ge(pe.h, v_sb)
            wait_res(SE, f"bcs_rd{bp}")
            v_bse[bp] = ac.inc(SE.copy(bc_s[bp][:], pb[4][:, :]))
            set_res("pb4g_rd", (ac, v_bse[bp]))

        emit_sbcast(0)
        for e in range(NEX):
            ep = e % 2
            hb = ep * 8
            v_hm = {}
            PRD = GE if POOL_H_PROD else VE
            sem_prd = gp if POOL_H_PROD else ve
            for hc in range(HCN):
                par = hc % 2
                ws = con_slot("wbig")
                q_wait(TE)
                wait_res(TE, f"pbh1{par}_rd")
                wait_res(TE, f"pbh3{par}_rd")
                ins = None
                for kc in range(C):
                    TE.matmul(pb[par][:, :], wbig[ws][:, kc * 128:(kc + 1) * 128],
                              actc(kc), start=(kc == 0), stop=(kc == C - 1))
                    ins = TE.matmul(pb[2 + par][:, :],
                                    wbig[ws][:, D + kc * 128:D + (kc + 1) * 128],
                                    actc(kc), start=(kc == 0), stop=(kc == C - 1))
                v_h = pe.inc(ins)
                free_slot("wbig", pe, v_h)
                q_pump(6)
                if e < 8:
                    VE.wait_ge(pe.h, v_h)
                    VE.wait_ge(ac.h, v_bse[ep])
                    wait_res(VE, f"fscr{par}0_rd")
                    v_h1s = ve.inc(VE.tensor_tensor(scrf(par, 0), pb[par][:, :],
                                                    bc_s[ep][:], op=ALU.mult))
                    set_res(f"pbh1{par}_rd", (ve, v_h1s))
                    SE.wait_ge(ve.h, v_h1s)
                    wait_res(SE, f"fscr{par}1_rd")
                    v_silu = ac.inc(SE.activation(scrf(par, 1), scrf(par, 0), AF.Silu))
                    set_res(f"fscr{par}0_rd", (ac, v_silu))
                    wait_res(VE, f"fscr{par}2_rd")
                    v_h3s = ve.inc(VE.tensor_tensor(scrf(par, 2), pb[2 + par][:, :],
                                                    bc_s[ep][:], op=ALU.mult))
                    set_res(f"pbh3{par}_rd", (ve, v_h3s))
                    v_h3s_last = v_h3s
                    PRD.wait_ge(ac.h, v_silu)
                    if POOL_H_PROD:
                        PRD.wait_ge(ve.h, v_h3s)
                    wait_res(PRD, f"h_rd{ep}")
                    v_m = sem_prd.inc(PRD.tensor_tensor(combc(hb + hc), scrf(par, 1),
                                                        scrf(par, 2), op=ALU.mult))
                    set_res(f"fscr{par}1_rd", (sem_prd, v_m))
                    set_res(f"fscr{par}2_rd", (sem_prd, v_m))
                else:
                    SE.wait_ge(pe.h, v_h)
                    wait_res(SE, f"fscr{par}1_rd")
                    v_silu = ac.inc(SE.activation(scrf(par, 1), pb[par][:, :], AF.Silu))
                    set_res(f"pbh1{par}_rd", (ac, v_silu))
                    VE.wait_ge(ac.h, v_silu)
                    wait_res(VE, f"h_rd{ep}")
                    v_m = ve.inc(VE.tensor_tensor(combc(hb + hc), scrf(par, 1),
                                                  pb[2 + par][:, :], op=ALU.mult))
                    set_res(f"pbh3{par}_rd", (ve, v_m))
                    set_res(f"fscr{par}1_rd", (ve, v_m))
                v_hm[hc] = (sem_prd if e < 8 else ve, v_m)
            if e < 8:
                set_res(f"bcs_rd{ep}", (ve, v_h3s_last))
            if e + 1 < 8:
                emit_sbcast(e + 1)
            v_w2_last = None
            for fc in range(C):
                par2 = fc % 2
                w2s = con_slot("w2b")
                q_wait(TE)
                wait_res(TE, f"pbw2{par2}_rd")
                ins = None
                for kc in range(HCN):
                    if fc == 0:
                        sem_h, v_h_ = v_hm[kc]
                        TE.wait_ge(sem_h.h, v_h_)
                    ins = TE.matmul(pb[5 + par2][:, :], w2b[w2s][:, kc * 128:(kc + 1) * 128],
                                    combc(hb + kc), start=(kc == 0), stop=(kc == HCN - 1))
                v_w2 = pe.inc(ins)
                free_slot("w2b", pe, v_w2)
                q_pump(6)
                if e == 0:
                    SE.wait_ge(pe.h, v_w2)
                    v_ev = ac.inc(SE.copy(s1c(fc), pb[5 + par2][:, :]))
                    set_res(f"pbw2{par2}_rd", (ac, v_ev))
                    set_res(f"s1w_{fc}", (ac, v_ev))
                elif POOL_W2_EVAC and fc % 2 == 1:
                    SE.wait_ge(pe.h, v_w2)
                    wait_res(SE, "bcb_rd")
                    v_cp = ac.inc(SE.copy(bc_b[:], pb[5 + par2][:, :]))
                    set_res(f"pbw2{par2}_rd", (ac, v_cp))
                    GE.wait_ge(ac.h, v_cp)
                    wait_res(GE, f"s1w_{fc}")
                    v_ev = gp.inc(GE.tensor_tensor(s1c(fc), bc_b[:], s1c(fc), op=ALU.add))
                    set_res("bcb_rd", (gp, v_ev))
                    set_res(f"s1w_{fc}", (gp, v_ev))
                else:
                    VE.wait_ge(pe.h, v_w2)
                    wait_res(VE, f"s1w_{fc}")
                    v_ev = ve.inc(VE.scalar_tensor_tensor(s1c(fc), pb[5 + par2][:, :], 0.0,
                                                          s1c(fc), op0=ALU.add, op1=ALU.add))
                    set_res(f"pbw2{par2}_rd", (ve, v_ev))
                    set_res(f"s1w_{fc}", (ve, v_ev))
                if e == NEX - 1 and fc >= 2:
                    ln_stat_chunk(fc - 2, s1c(fc - 2), ready=res[f"s1w_{fc - 2}"][0])
                v_w2_last = v_w2
            if e == NEX - 1:
                for fc in (C - 2, C - 1):
                    v = ln_stat_chunk(fc, s1c(fc), ready=res[f"s1w_{fc}"][0])
                    if v is not None:
                        v_stats = v
            set_res(f"h_rd{ep}", (pe, v_w2_last))
        return v_stats

    def dump(name, src, dst=None):
        if not debug:
            return
        for sm in (pe, ve, ac, gp):
            SY.wait_ge(sm.h, sm.n)
        qo.inc(SY.dma_start(out=dbg[name] if dst is None else dst, in_=src), 16)
        v = qo.n
        for e in (TE, VE, SE):
            e.wait_ge(qo.h, v)

    out_v = out_h.rearrange("(c p) t -> p c t", p=128)

    def store_sub(s):
        qo.inc(SY.dma_start(out=out_v[:, 0:8, s * ST:(s + 1) * ST],
                            in_=x_sb_v[:, 0:8, s * ST:(s + 1) * ST]), 16)
        qo.inc(SY.dma_start(out=out_v[:, 8:16, s * ST:(s + 1) * ST],
                            in_=x_sb_v[:, 8:16, s * ST:(s + 1) * ST]), 16)

    pending = None
    for s in range(NS):
        for h in range(H):
            q_register(*mk_head_unit(h))
        for fc in range(C):
            q_register(*mk_wo_unit(fc))
        g13 = 0
        g2 = 0
        for e in range(NEX):
            for hc in range(HCN):
                q_register(*mk_w13_unit(e, hc, g13))
                g13 += 1
            for fc in range(C):
                q_register(*mk_w2_unit(e, fc, g2))
                g2 += 1
        if s == 0:
            SE.wait_ge(qx.h, v_x)
        q_pump(6)

        if s == 0:
            dump("dbg_x0", x_sb[:])
            dump("dbg_wa", wa_sb[:])
            dump("dbg_waT", waT_sb[:])
            dump("dbg_sm", sm_sb[:])
            dump("dbg_s4", small4[:])
        v1 = emit_ln_stats(lambda c: xc(s, c))
        ap1 = emit_ln_finish(s, lambda c: xc(s, c), "pre", v1,
                             bct=(bc_s[0], bc_s[1]) if pending is not None else None)
        if s == 0:
            dump("dbg_xn", act[:])
        v_comb = emit_attn(s, q0_ready=ap1)
        if s == 0:
            dump("dbg_comb", comb[:])
        guard = None
        if pending is not None:
            sp, v_bcev_p = pending
            guard = emit_ln_applies_deferred(sp, v_bcev_p)
            SY.wait_ge(ve.h, guard[-1])
            store_sub(sp)
            pending = None
        v2 = emit_wo(s, v_comb, s1_guard=guard)
        if s == 0:
            dump("dbg_attn", s1[:])
        ap2 = emit_ln_finish(s, s1c, "post", v2)
        if s == 0 and debug:
            dump("dbg_x1", x_sb[:].rearrange("p (c t) -> p c t", c=C)[:, :, 0:ST],
                 dst=dbg["dbg_x1"].rearrange("p (c t) -> p c t", c=C))
        v3 = emit_ln_stats(lambda c: xc(s, c), src_ready=ap2)
        ap3 = emit_ln_finish(s, lambda c: xc(s, c), "pre", v3)
        if s == 0:
            dump("dbg_y", act[:])
        v_sfm = emit_gate(s, ready=ap3)
        if s == 0:
            dump("dbg_scores", scores[:])
            dump("dbg_sfm", sfm[:])
            dump("dbg_stm", stm[:])
            dump("dbg_scl", scl[:])
            dump("dbg_m01", m01[:])
        v4 = emit_ffn(s, v_sfm)
        if s == 0:
            dump("dbg_ffn", s1[:])
        if s + 1 < NS:
            pending = (s, emit_ln_finish(s, s1c, "post", v4, defer=True))
        else:
            emit_ln_finish(s, s1c, "post", v4)
            SY.wait_ge(ve.h, ve.n)
            SY.wait_ge(gp.h, gp.n)
            store_sub(s)

    SY.wait_ge(qo.h, qo.n)
    return nc


def _swz2(w):
    a, b = w.shape
    return np.ascontiguousarray(
        w.reshape(a // 128, 128, b // 128, 128).transpose(2, 1, 0, 3)
    ).reshape(b // 128, 128, a)


def kernel(**inputs):
    f32 = np.float32
    x = np.asarray(inputs["L0_attn_hidden_states"], dtype=f32)
    Wq = np.asarray(inputs["L0_attn_Wq"], dtype=f32)
    Wo = np.asarray(inputs["L0_attn_Wo"], dtype=f32)
    bw = np.asarray(inputs["L0_attn_bank_weights"], dtype=f32)
    mem = np.asarray(inputs["L0_attn_memories"], dtype=f32)
    mnorm = np.asarray(inputs["L0_attn_memory_norms"], dtype=f32)
    gw = np.asarray(inputs["L0_ffn_gate_w"], dtype=f32)
    w1 = np.asarray(inputs["L0_ffn_w1"], dtype=f32)
    w2 = np.asarray(inputs["L0_ffn_w2"], dtype=f32)
    w3 = np.asarray(inputs["L0_ffn_w3"], dtype=f32)
    ws1 = np.asarray(inputs["L0_ffn_ws1"], dtype=f32)
    ws2 = np.asarray(inputs["L0_ffn_ws2"], dtype=f32)
    ws3 = np.asarray(inputs["L0_ffn_ws3"], dtype=f32)

    xf = np.ascontiguousarray(x.reshape(NCORES * TCORE, D))
    xT_cores = [np.ascontiguousarray(xf[i * TCORE:(i + 1) * TCORE].T) for i in range(NCORES)]

    wq_host = _swz2(np.ascontiguousarray(Wq.T))
    wo_host = _swz2(np.ascontiguousarray(Wo.T))

    def swz_e(w_e):
        return _swz2(np.ascontiguousarray(w_e.T))

    w1_host = np.stack([swz_e(w1[e]) for e in range(8)] + [swz_e(ws1)])
    w3_host = np.stack([swz_e(w3[e]) for e in range(8)] + [swz_e(ws3)])
    w2_host = np.stack([swz_e(w2[e]) for e in range(8)] + [swz_e(ws2)])

    gw_host = np.ascontiguousarray(
        gw.T.reshape(16, 128, 8).transpose(1, 0, 2)).reshape(128, 128)
    mh_host = np.ascontiguousarray(np.concatenate(
        [mem.transpose(2, 1, 0, 3).reshape(128, H, NB * 128),
         mnorm.transpose(2, 1, 0)], axis=2))
    bw_host = np.ascontiguousarray(bw)
    nf_host = np.ascontiguousarray(mnorm.reshape(NB, H * 128))

    nc = build_program()
    shared = {
        "wq_h": wq_host, "wo_h": wo_host, "mh_h": mh_host,
        "w1_h": w1_host, "w3_h": w3_host, "w2_h": w2_host,
        "gw_h": gw_host, "bw_h": bw_host, "nf_h": nf_host,
    }
    in_maps = [dict(shared, xT=xT_cores[i]) for i in range(NCORES)]
    res = run_bass_kernel_spmd(nc, in_maps, core_ids=list(range(NCORES)))
    out = np.concatenate([np.asarray(r["out"]).T for r in res.results], axis=0)
    return np.ascontiguousarray(out.reshape(4, 2048, 2048)).astype(np.float32)


# version 35
# speedup vs baseline: 1.0023x; 1.0023x over previous
from collections import deque
from contextlib import ExitStack

import numpy as np

import concourse.bass as bass
import concourse.mybir as mybir
from concourse.bass_utils import run_bass_kernel_spmd

F32 = mybir.dt.float32
BF16 = mybir.dt.bfloat16
AF = mybir.ActivationFunctionType
ALU = mybir.AluOpType
AX = mybir.AxisListType

NCORES = 8
D = 2048
C = 16
TCORE = 1024
ST = 512
NS = 2
H = 16
NB = 4
HCN = 8
NEX = 9
EPS_LN = 1e-5
EPS_ATT = 1e-6
INV_D = 1.0 / D

FENCE_SE = True
FENCE_GE = True
POOL_ELU_ADD = True
POOL_A_REDUCE = False
POOL_LN_APPLY = True
POOL_LN_CAST = True
POOL_H_PROD = True
POOL_W2_EVAC = False


class Sem:
    __slots__ = ("h", "n")

    def __init__(self, h):
        self.h = h
        self.n = 0

    def inc(self, instr, k=1):
        instr.then_inc(self.h, k)
        self.n += k
        return self.n


class AutoSem(Sem):

    def inc(self, instr, k=1):
        return self.n


_AUTO_OPS = {
    "tensor_tensor", "tensor_scalar", "tensor_scalar_mul", "tensor_scalar_min",
    "tensor_scalar_max", "tensor_scalar_add", "scalar_tensor_tensor",
    "tensor_copy", "tensor_add", "tensor_sub", "tensor_mul", "memset",
    "reduce_max", "reduce_sum", "tensor_reduce", "reciprocal", "max",
    "activation", "copy", "square", "iota", "select",
}


class _Counting:

    def __init__(self, eng, sem):
        self._eng = eng
        self._sem = sem

    def __getattr__(self, name):
        attr = getattr(self._eng, name)
        if name in _AUTO_OPS and callable(attr):
            sem = self._sem

            def call(*a, **k):
                ins = attr(*a, **k)
                ins.then_inc(sem.h, 1)
                sem.n += 1
                return ins
            return call
        return attr


def _ap_ranges(args, kwargs):
    aps = []
    for x in args:
        if hasattr(x, "tensor") and hasattr(x, "offset"):
            aps.append(x)
    for x in kwargs.values():
        if hasattr(x, "tensor") and hasattr(x, "offset"):
            aps.append(x)

    def ext(ap):
        try:
            span = 1
            for step, count in ap.ap:
                span += abs(int(step)) * (int(count) - 1)
            off = int(ap.offset)
            return (ap.tensor.name, off, off + span)
        except Exception:
            return (ap.tensor.name, 0, 1 << 60)
    return ([ext(aps[0])] if aps else []), [ext(x) for x in aps[1:]]


def _overlap(r1, r2):
    for n1, s1, e1 in r1:
        for n2, s2, e2 in r2:
            if n1 == n2 and s1 < e2 and s2 < e1:
                return True
    return False


class AutoFenced:

    def __init__(self, eng, sem):
        self._eng = eng
        self._sem = sem
        self._recent = []
        self._covered = 0

    def __getattr__(self, name):
        attr = getattr(self._eng, name)
        if name in _AUTO_OPS and callable(attr):
            eng, sem = self._eng, self._sem

            def call(*a, **k):
                w, r = _ap_ranges(a, k)
                need = 0
                for val, pw, pr in self._recent:
                    if _overlap(r, pw) or _overlap(w, pw) or _overlap(w, pr):
                        need = max(need, val)
                if need > self._covered:
                    eng.wait_ge(sem.h, need)
                    self._covered = need
                    self._recent = [t for t in self._recent if t[0] > need]
                ins = attr(*a, **k)
                ins.then_inc(sem.h, 1)
                sem.n += 1
                self._recent.append((sem.n, w, r))
                if len(self._recent) > 24:
                    v = self._recent[0][0]
                    eng.wait_ge(sem.h, v)
                    self._covered = v
                    self._recent = [t for t in self._recent if t[0] > v]
                return ins
            return call
        return attr

    def wait_covered(self, val):
        if val > self._covered:
            self._covered = val
            self._recent = [t for t in self._recent if t[0] > val]


def build_program(debug=False):
    nc = bass.Bass("TRN2", target_bir_lowering=False)
    stack = ExitStack()
    nc._keepalive_stack = stack

    def sb(name, shape, dt):
        return stack.enter_context(nc.sbuf_tensor(name, shape, dt))

    xT = nc.dram_tensor("xT", [D, TCORE], F32, kind="ExternalInput").ap()
    wq_h = nc.dram_tensor("wq_h", [H, 128, D], F32, kind="ExternalInput").ap()
    wo_h = nc.dram_tensor("wo_h", [C, 128, D], F32, kind="ExternalInput").ap()
    mh_h = nc.dram_tensor("mh_h", [128, H, 516], F32, kind="ExternalInput").ap()
    w1_h = nc.dram_tensor("w1_h", [NEX, HCN, 128, D], F32, kind="ExternalInput").ap()
    w3_h = nc.dram_tensor("w3_h", [NEX, HCN, 128, D], F32, kind="ExternalInput").ap()
    w2_h = nc.dram_tensor("w2_h", [NEX, C, 128, 1024], F32, kind="ExternalInput").ap()
    gw_h = nc.dram_tensor("gw_h", [128, 128], F32, kind="ExternalInput").ap()
    bw_h = nc.dram_tensor("bw_h", [H, NB], F32, kind="ExternalInput").ap()
    nf_h = nc.dram_tensor("nf_h", [NB, 2048], F32, kind="ExternalInput").ap()
    out_h = nc.dram_tensor("out", [D, TCORE], F32, kind="ExternalOutput").ap()
    dbg = {}
    if debug:
        for nm, shape, dt in [
            ("dbg_x0", [128, C * TCORE], F32), ("dbg_xn", [128, C * ST], BF16),
            ("dbg_comb", [128, C * ST], BF16), ("dbg_attn", [128, C * ST], F32),
            ("dbg_x1", [128, C * ST], F32), ("dbg_y", [128, C * ST], BF16),
            ("dbg_scores", [8, ST], F32), ("dbg_sfm", [8, ST], F32),
            ("dbg_ffn", [128, C * ST], F32), ("dbg_wa", [H, NB], F32),
            ("dbg_waT", [NB, H], F32), ("dbg_sm", [H, NB], F32),
            ("dbg_s4", [NB, 4], F32), ("dbg_stm", [128, 32], F32),
            ("dbg_scl", [128, 32], F32), ("dbg_m01", [128, 8], F32),
        ]:
            dbg[nm] = nc.dram_tensor(nm, shape, dt, kind="ExternalOutput").ap()

    x_sb = sb("x_sb", [128, C * TCORE], F32)
    s1 = sb("s1", [128, C * ST], F32)
    act = sb("act", [128, C * ST], BF16)
    comb = sb("comb", [128, C * ST], BF16)
    NWB = 3
    wbig = [sb(f"wbig{i}", [128, 2 * D], BF16) for i in range(NWB)]
    w2b = [sb(f"w2b{i}", [128, 1024], BF16) for i in range(3)]
    mhb = [sb(f"mhb{i}", [128, 516], BF16) for i in range(6)]
    sig = [sb(f"sig{i}", [128, ST], BF16) for i in range(3)]
    scrF = sb("scrF", [128, 6 * ST], F32)

    def scrf(par, i):
        return scrF[:, (par * 3 + i) * ST:(par * 3 + i + 1) * ST]

    scrA = [sb(f"scrA{i}", [128, ST], F32) for i in range(2)]
    sgs = [sb(f"sgs{i}", [128, ST], BF16) for i in range(2)]
    sel4 = sb("sel4", [NB, NB * 128], BF16)
    c4b = [sb(f"c4b{i}", [NB, ST], BF16) for i in range(2)]
    scrN = sb("scrN", [NB, 2 * ST], F32)

    def scrNv(par):
        return scrN[:, par * ST:(par + 1) * ST]

    bc_a = sb("bc_a", [128, ST], F32)
    bc_b = sb("bc_b", [128, ST], F32)
    bc_s = [sb(f"bc_s{i}", [128, ST], F32) for i in range(2)]
    gw_sb = sb("gw_sb", [128, 128], BF16)
    bw_sb = sb("bw_sb", [H, NB], F32)
    sm_sb = sb("sm_sb", [H, NB], F32)
    wa_sb = sb("wa_sb", [H, NB], F32)
    waT_sb = sb("waT_sb", [NB, H], F32)
    small16 = sb("small16", [H, 8], F32)
    small4 = sb("small4", [NB, 4], F32)
    act_row = sb("act_row", [1, 4], F32)
    nf_sb = scrF[0:4, 0:2048]
    scores = sb("scores", [8, ST], F32)
    sfm = sb("sfm", [8, ST], F32)
    stm = sb("stm", [128, 32], F32)
    mask0 = sb("mask0", [128, 32], F32)
    stm2 = sb("stm2", [128, 32], F32)
    mask1 = sb("mask1", [128, 32], F32)
    scl = sb("scl", [128, 32], F32)
    m01 = sb("m01", [128, 8], F32)
    selrow = sb("selrow", [8, 128], F32)
    ident = sb("ident", [128, 128], F32)
    iota_r = sb("iota_r", [128, 128], F32)
    iota_c = sb("iota_c", [128, 1], F32)
    ones_col_bf = sb("ones_col_bf", [128, 1], BF16)
    ones_col_f = sb("ones_col_f", [128, 1], F32)
    ones_row_f = sb("ones_row_f", [1, 128], F32)
    consts = sb("consts", [128, 8], F32)

    pb = [stack.enter_context(nc.psum_tensor(f"pb{i}", [128, 512], F32)) for i in range(8)]

    qx4 = [Sem(stack.enter_context(nc.semaphore(f"qx{g}"))) for g in range(4)]
    qs = Sem(stack.enter_context(nc.semaphore("qs")))
    qg = Sem(stack.enter_context(nc.semaphore("qg")))
    qwp = [Sem(stack.enter_context(nc.semaphore(f"qw{i}"))) for i in range(16)]
    qo = Sem(stack.enter_context(nc.semaphore("qo")))
    pe = Sem(stack.enter_context(nc.semaphore("pe")))
    ve = AutoSem(stack.enter_context(nc.semaphore("ve")))
    ac = AutoSem(stack.enter_context(nc.semaphore("ac")))
    gp = AutoSem(stack.enter_context(nc.semaphore("gp")))

    TE = nc.tensor
    VE = AutoFenced(nc.vector, ve)
    SE = AutoFenced(nc.scalar, ac) if FENCE_SE else _Counting(nc.scalar, ac)
    GE = AutoFenced(nc.gpsimd, gp) if FENCE_GE else _Counting(nc.gpsimd, gp)
    SY = nc.sync

    def join(*engs):
        for e in engs:
            for s in (pe, ve, ac, gp):
                e.wait_ge(s.h, s.n)

    def ve_sync(instr):
        return ve.n

    def xc(s, c):
        return x_sb[:, c * TCORE + s * ST: c * TCORE + s * ST + ST]

    def actc(c):
        return act[:, c * ST:(c + 1) * ST]

    def combc(c):
        return comb[:, c * ST:(c + 1) * ST]

    def s1c(c):
        return s1[:, c * ST:(c + 1) * ST]

    r_su = scrA[0][0:1, :]
    r_msq = scrA[0][0:1, :]
    r_sq = scrA[1][0:1, :]
    r_mean = scrN[0:1, 0:ST]
    r_var = scrN[0:1, ST:2 * ST]
    r_rstd = bc_s[0][0:1, :]
    r_b = sfm[0:1, :]

    res = {}

    def wait_res(eng, key):
        for s, v in res.get(key, ()):
            eng.wait_ge(s.h, v)

    def set_res(key, *pairs):
        res[key] = list(pairs)

    NBUF = {"wbig": NWB, "mhb": 6, "w2b": 3}
    wreg = {k: 0 for k in NBUF}
    wcon = {k: 0 for k in NBUF}
    free_events = {k: [] for k in NBUF}

    def reg_slot(cls):
        s_ = wreg[cls] % NBUF[cls]
        wreg[cls] += 1
        return s_

    def con_slot(cls):
        s_ = wcon[cls] % NBUF[cls]
        wcon[cls] += 1
        return s_

    def free_slot(cls, sem, val):
        free_events[cls].append((sem, val))

    def slot_wait(cls, seq):
        if seq >= NBUF[cls]:
            sem, val = free_events[cls][seq - NBUF[cls]]
            GE.wait_ge(sem.h, val)

    def slot_ready(cls, seq):
        return seq < NBUF[cls] or len(free_events[cls]) > seq - NBUF[cls]

    units = []
    classQ = {c: [] for c in NBUF}
    cls_issued = {c: 0 for c in NBUF}
    cls_consumed = {c: 0 for c in NBUF}
    consumed = [0]
    issue_seq = [0]

    def q_register(cls, ready_fn, issue_fn):
        units.append((cls, len(classQ[cls])))
        classQ[cls].append([ready_fn, issue_fn, None])

    def q_pump(k):
        for cls in NBUF:
            q = classQ[cls]
            while (cls_issued[cls] < len(q)
                   and cls_issued[cls] < cls_consumed[cls] + k
                   and q[cls_issued[cls]][0]()):
                rec = q[cls_issued[cls]]
                sem = qwp[issue_seq[0] % 16]
                issue_seq[0] += 1
                GE.wait_ge(sem.h, sem.n)
                rec[1](sem)
                rec[2] = (sem, sem.n)
                cls_issued[cls] += 1

    def q_wait(eng):
        cls, seq = units[consumed[0]]
        consumed[0] += 1
        cls_consumed[cls] += 1
        while classQ[cls][seq][2] is None:
            q_pump(3)
        sem, v = classQ[cls][seq][2]
        eng.wait_ge(sem.h, v)

    def mk_head_unit(h):
        wseq = wreg["wbig"]
        mseq = wreg["mhb"]
        ws = reg_slot("wbig")
        ms = reg_slot("mhb")

        def ready():
            return slot_ready("wbig", wseq) and slot_ready("mhb", mseq)

        def fn(sem):
            slot_wait("wbig", wseq)
            sem.inc(GE.dma_start(out=wbig[ws][:, 0:D], in_=wq_h[h]), 16)
            slot_wait("mhb", mseq)
            sem.inc(GE.dma_start(out=mhb[ms][:], in_=mh_h[:, h, :]), 16)
        return "wbig", ready, fn

    def mk_wo_unit(fc):
        wseq = wreg["wbig"]
        ws = reg_slot("wbig")

        def ready():
            return slot_ready("wbig", wseq)

        def fn(sem):
            slot_wait("wbig", wseq)
            sem.inc(GE.dma_start(out=wbig[ws][:, 0:D], in_=wo_h[fc]), 16)
        return "wbig", ready, fn

    def mk_w13_unit(e, hc, g):
        wseq = wreg["wbig"]
        ws = reg_slot("wbig")

        def ready():
            return slot_ready("wbig", wseq)

        def fn(sem):
            slot_wait("wbig", wseq)
            sem.inc(GE.dma_start(out=wbig[ws][:, 0:D], in_=w1_h[e, hc]), 16)
            sem.inc(GE.dma_start(out=wbig[ws][:, D:2 * D], in_=w3_h[e, hc]), 16)
        return "wbig", ready, fn

    def mk_w2_unit(e, fc, g):
        w2seq = wreg["w2b"]
        w2s = reg_slot("w2b")

        def ready():
            return slot_ready("w2b", w2seq)

        def fn(sem):
            slot_wait("w2b", w2seq)
            sem.inc(GE.dma_start(out=w2b[w2s][:], in_=w2_h[e, fc]), 16)
        return "w2b", ready, fn

    xT_v = xT.rearrange("(c p) t -> p c t", p=128)
    x_sb_v = x_sb[:].rearrange("p (c t) -> p c t", c=C)
    for g in range(4):
        qx4[g].inc(SY.dma_start(out=x_sb_v[:, 4 * g:4 * g + 4, :],
                                in_=xT_v[:, 4 * g:4 * g + 4, :]), 16)
    qs.inc(SY.dma_start(out=bw_sb[:], in_=bw_h), 16)
    qs.inc(SY.dma_start(out=nf_sb[:], in_=nf_h), 16)
    qg.inc(GE.dma_start(out=gw_sb[:], in_=gw_h), 16)
    v_small = qs.n

    GE.iota(iota_r[:], pattern=[[1, 128]], base=0, channel_multiplier=0,
            allow_small_or_imprecise_dtypes=True)
    v_iota = gp.inc(GE.iota(iota_c[:], pattern=[[0, 1]], base=0, channel_multiplier=1,
                            allow_small_or_imprecise_dtypes=True))

    VE.memset(ones_col_bf[:], 1.0)
    VE.memset(ones_col_f[:], 1.0)
    VE.memset(ones_row_f[:], 1.0)
    VE.memset(consts[:, 0:1], EPS_LN)
    VE.memset(consts[:, 1:2], -0.5)
    VE.memset(consts[:, 2:3], -1.0)
    VE.wait_ge(gp.h, v_iota)
    VE.tensor_scalar(ident[:], iota_r[:], iota_c[:, 0:1], None, op0=ALU.is_equal)
    for n in range(NB):
        VE.tensor_scalar(sel4[:, n * 128:(n + 1) * 128],
                         iota_c[0:NB, 0:1].to_broadcast([NB, 128]), float(n), None,
                         op0=ALU.is_equal)

    VE.wait_ge(qs.h, v_small)
    ve_sync(VE.reduce_sum(small4[:, 0:1], nf_sb[:], axis=AX.X))
    VE.tensor_scalar(small4[:, 1:2], small4[:, 0:1], EPS_ATT, None, op0=ALU.is_ge)
    VE.wait_ge(qs.h, v_small)
    ve_sync(VE.reduce_max(small16[:, 0:1], bw_sb[:], axis=AX.X))
    v2 = ve.inc(VE.tensor_tensor(sm_sb[:], bw_sb[:],
                                 small16[:, 0:1].to_broadcast([H, NB]), op=ALU.subtract))
    SE.wait_ge(ve.h, v2)
    v3 = ac.inc(SE.activation(sm_sb[:], sm_sb[:], AF.Exp))
    VE.wait_ge(ac.h, v3)
    ve_sync(VE.reduce_sum(small16[:, 1:2], sm_sb[:], axis=AX.X))
    ve_sync(VE.reciprocal(small16[:, 2:3], small16[:, 1:2]))
    v4 = ve.inc(VE.tensor_tensor(sm_sb[:], sm_sb[:],
                                 small16[:, 2:3].to_broadcast([H, NB]), op=ALU.mult))
    TE.wait_ge(ve.h, v4)
    v5 = pe.inc(TE.transpose(pb[6][0:1, 0:4], small4[0:4, 1:2], ident[0:4, 0:4]))
    SE.wait_ge(pe.h, v5)
    v6 = ac.inc(SE.copy(act_row[:], pb[6][0:1, 0:4]))
    TE.wait_ge(ac.h, v6)
    v7 = pe.inc(TE.matmul(pb[7][0:16, 0:4], ones_row_f[0:1, 0:16], act_row[:],
                          start=True, stop=True))
    VE.wait_ge(pe.h, v7)
    v8 = ve.inc(VE.tensor_tensor(wa_sb[:], pb[7][0:16, 0:4], sm_sb[:], op=ALU.mult))
    TE.wait_ge(ve.h, v8)
    v9 = pe.inc(TE.transpose(pb[6][0:4, 0:16], wa_sb[:], ident[0:16, 0:16]))
    SE.wait_ge(pe.h, v9)
    ac.inc(SE.copy(waT_sb[:], pb[6][0:4, 0:16]))

    def ln_stat_chunk(c, src_ap, ready=None):
        if c % 4 == 0:
            TE.wait_ge(qx4[c // 4].h, 16)
            SE.wait_ge(qx4[c // 4].h, 16)
        if c == 0:
            wait_res(TE, "pbn0_rd")
            wait_res(TE, "pbn1_rd")
        if ready is not None:
            TE.wait_ge(ready[0].h, ready[1])
            SE.wait_ge(ready[0].h, ready[1])
        TE.matmul(pb[2][0:1, :], ones_col_f[:], src_ap,
                  start=(c == 0), stop=(c == C - 1))
        v_sq = ac.inc(SE.activation(actc(c), src_ap, AF.Square))
        TE.wait_ge(ac.h, v_sq)
        ins = TE.matmul(pb[3][0:1, :], ones_col_bf[:], actc(c),
                        start=(c == 0), stop=(c == C - 1))
        return pe.inc(ins) if c == C - 1 else None

    def emit_ln_stats(src_fn, src_ready=None):
        v_stats = None
        for c in range(C):
            r = src_ready[c] if src_ready is not None else None
            v = ln_stat_chunk(c, src_fn(c), ready=r)
            if v is not None:
                v_stats = v
        return v_stats

    def emit_ln_finish(s, src_fn, mode, v_stats, bct=None, defer=False):
        bca_t, bcb_t = bct if bct is not None else (bc_a, bc_b)
        SE.wait_ge(pe.h, v_stats)
        SE.copy(r_su, pb[2][0:1, :])
        v_ev = ac.inc(SE.copy(r_sq, pb[3][0:1, :]))
        TE.wait_ge(ac.h, v_ev)
        TE.matmul(pb[2][0:1, :], sel4[0:4, 0:1], sel4[0:4, :], start=True, stop=True)
        set_res("pbn0_rd", (ac, v_ev))
        set_res("pbn1_rd", (ac, v_ev))
        VE.wait_ge(ac.h, v_ev)
        ve_sync(VE.tensor_scalar_mul(r_mean, r_su, INV_D))
        VE.tensor_tensor(r_msq, r_mean, r_mean, op=ALU.mult)
        v_var = ve.inc(VE.scalar_tensor_tensor(r_var, r_sq, INV_D, r_msq,
                                               op0=ALU.mult, op1=ALU.subtract))
        TE.wait_ge(ve.h, v_var)
        TE.matmul(pb[2][0:1, :], sel4[0:4, 0:1], sel4[0:4, :], start=True, stop=True)
        SE.wait_ge(ve.h, v_var)
        SE.activation(r_var, r_var, AF.Ln, bias=consts[0:1, 0:1])
        v_rstd = ac.inc(SE.activation(r_rstd, r_var, AF.Exp, scale=consts[0:1, 1:2]))
        TE.wait_ge(ac.h, v_rstd)
        TE.matmul(pb[2][0:1, :], sel4[0:4, 0:1], sel4[0:4, :], start=True, stop=True)
        VE.wait_ge(ac.h, v_rstd)
        v_b = ve.inc(VE.scalar_tensor_tensor(r_b, r_mean, -1.0, r_rstd,
                                             op0=ALU.mult, op1=ALU.mult))
        TE.wait_ge(ve.h, v_b)
        for key in ("pbbc_rd", "pbcb0_rd", "pbcb1_rd", "pb4g_rd", "pbw20_rd"):
            wait_res(TE, key)
        TE.matmul(pb[4][:, :], ones_row_f[:], r_rstd, start=True, stop=True)
        v_bc = pe.inc(TE.matmul(pb[5][:, :], ones_row_f[:], r_b, start=True, stop=True))
        SE.wait_ge(pe.h, v_bc)
        if bct is not None:
            wait_res(SE, "bcs_rd0")
            wait_res(SE, "bcs_rd1")
        SE.copy(bca_t[:], pb[4][:, :])
        v_bcev = ac.inc(SE.copy(bcb_t[:], pb[5][:, :]))
        set_res("pbbc_rd", (ac, v_bcev))
        if defer:
            return v_bcev
        VE.wait_ge(ac.h, v_bcev)
        applies = []
        for c in range(C):
            VE.tensor_tensor(scrf(0, 0), src_fn(c), bca_t[:], op=ALU.mult)
            if mode == "pre":
                ins = VE.tensor_tensor(actc(c), scrf(0, 0), bcb_t[:], op=ALU.add)
            else:
                VE.tensor_tensor(scrf(0, 1), scrf(0, 0), bcb_t[:], op=ALU.add)
                ins = VE.tensor_tensor(xc(s, c), scrf(0, 1), xc(s, c), op=ALU.add)
            applies.append((ve, ve.inc(ins)))
        if bct is not None:
            set_res("bcs_rd0", (ve, applies[-1][1]))
            set_res("bcs_rd1", (ve, applies[-1][1]))
        return applies

    def emit_ln_applies_deferred(s, v_bcev):
        wait_res(VE, "scrf00_rd")
        wait_res(VE, "scrf012_rd")
        VE.wait_ge(ac.h, v_bcev)
        aps = []
        for c in range(C):
            VE.tensor_tensor(scrf(0, 0), s1c(c), bc_a[:], op=ALU.mult)
            VE.tensor_tensor(scrf(0, 1), scrf(0, 0), bc_b[:], op=ALU.add)
            aps.append(ve.inc(VE.tensor_tensor(xc(s, c), scrf(0, 1), xc(s, c),
                                               op=ALU.add)))
        set_res("pbbc_rd", (ve, aps[-1]))
        set_res("scrf00_rd", (ve, aps[-1]))
        set_res("scrf012_rd", (ve, aps[-1]))
        return aps

    def emit_attn(s, q0_ready):
        qstate = {}
        estate = {}
        nstate = {}
        last_cev = [0]

        def emit_q(h):
            par = h % 2
            ws = con_slot("wbig")
            ms = con_slot("mhb")
            q_wait(TE)
            wait_res(TE, f"pbq{par}_rd")
            ins = None
            for kc in range(C):
                if h == 0:
                    TE.wait_ge(q0_ready[kc][0].h, q0_ready[kc][1])
                ins = TE.matmul(pb[par][:, :], wbig[ws][:, kc * 128:(kc + 1) * 128],
                                actc(kc), start=(kc == 0), stop=(kc == C - 1))
            v_q = pe.inc(ins)
            free_slot("wbig", pe, v_q)
            qstate[h] = (v_q, ms)

        def emit_elu(h):
            par = h % 2
            sg = h % 3
            v_q, ms = qstate.pop(h)
            VE.wait_ge(pe.h, v_q)
            wait_res(VE, f"scrf{par}0_rd")
            v_min = ve.inc(VE.tensor_scalar_min(scrf(par, 0), pb[par][:, :], 0.0))
            SE.wait_ge(ve.h, v_min)
            wait_res(SE, f"scrf{par}12_rd")
            v_exp = ac.inc(SE.activation(scrf(par, 1), scrf(par, 0), AF.Exp))
            v_relu = ac.inc(SE.activation(scrf(par, 2), pb[par][:, :], AF.Relu))
            set_res(f"pbq{par}_rd", (ve, v_min), (ac, v_relu))
            set_res(f"scrf{par}0_rd", (ac, v_exp))
            EADD = GE if POOL_ELU_ADD else VE
            sem_eadd = gp if POOL_ELU_ADD else ve
            EADD.wait_ge(ac.h, v_relu)
            wait_res(EADD, f"sig{sg}_rd")
            v_sig = sem_eadd.inc(EADD.tensor_tensor(sig[sg][:], scrf(par, 1),
                                                    scrf(par, 2), op=ALU.add))
            set_res(f"scrf{par}12_rd", (sem_eadd, v_sig))
            estate[h] = (ms, (sem_eadd, v_sig))

        def emit_nc(h):
            par = h % 2
            sg = h % 3
            ms, (sem_sig, v_sig) = estate.pop(h)
            TE.wait_ge(sem_sig.h, v_sig)
            wait_res(TE, f"pbn{par}_rd")
            v_n = pe.inc(TE.matmul(pb[2 + par][0:4, :], mhb[ms][:, 512:516],
                                   sig[sg][:], start=True, stop=True))
            VE.wait_ge(pe.h, v_n)
            wait_res(VE, f"scrN{par}_rd")
            v_nmax = ve.inc(VE.tensor_scalar_max(scrNv(par), pb[2 + par][0:4, :], EPS_ATT))
            SE.wait_ge(ve.h, v_nmax)
            SE.activation(scrNv(par), scrNv(par), AF.Ln)
            v_cexp = ac.inc(SE.activation(scrNv(par), scrNv(par), AF.Exp,
                                          scale=consts[0:NB, 2:3]))
            set_res(f"scrN{par}_rd", (ac, v_cexp))
            VE.wait_ge(ac.h, v_cexp)
            wait_res(VE, f"c4_{par}_rd")
            v_c4 = ve.inc(VE.tensor_tensor(c4b[par][:], scrNv(par),
                                           waT_sb[:, h:h + 1].to_broadcast([NB, ST]),
                                           op=ALU.mult))
            nstate[h] = (ms, v_c4, v_nmax, (sem_sig, v_sig))

        def emit_fin(h):
            par = h % 2
            sg = h % 3
            ms, v_c4, v_nmax, (sem_sig, v_sig) = nstate.pop(h)
            banks = [2 + par, 4, 5, 2 + par]
            v_cb = [None] * NB
            TE.wait_ge(ve.h, v_c4)
            wait_res(TE, "pbbc_rd")
            for n in range(3):
                if n > 0:
                    wait_res(TE, f"pbcb{n - 1}_rd")
                v_cb[n] = pe.inc(TE.matmul(pb[banks[n]][:, :],
                                           sel4[:, n * 128:(n + 1) * 128],
                                           c4b[par][:], start=True, stop=True))
            v_sg = [None] * NB
            v_A = None
            for n in range(NB):
                VE.wait_ge(pe.h, v_cb[n])
                if n == 0:
                    VE.wait_ge(sem_sig.h, v_sig)
                wait_res(VE, f"sgs{n % 2}_rd")
                v_sg[n] = ve.inc(VE.tensor_tensor(sgs[n % 2][:], sig[sg][:],
                                                  pb[banks[n]][:, :], op=ALU.mult))
                if n == 0:
                    TE.wait_ge(ve.h, v_sg[0])
                    v_cb[3] = pe.inc(TE.matmul(pb[banks[3]][:, :],
                                               sel4[:, 3 * 128:4 * 128],
                                               c4b[par][:], start=True, stop=True))
                TE.wait_ge(ve.h, v_sg[n])
                if n == 0:
                    wait_res(TE, f"pbA{par}_rd")
                v_A = pe.inc(TE.matmul(pb[6 + par][:, :], mhb[ms][:, n * 128:(n + 1) * 128],
                                       sgs[n % 2][:], start=(n == 0), stop=(n == NB - 1)))
                set_res(f"sgs{n % 2}_rd", (pe, v_A))
            free_slot("mhb", pe, v_A)
            set_res(f"sig{sg}_rd", (ve, v_sg[3]))
            set_res(f"c4_{par}_rd", (pe, v_cb[3]))
            set_res(f"pbn{par}_rd", (ve, v_sg[3]))
            set_res("pbcb0_rd", (ve, v_sg[1]))
            set_res("pbcb1_rd", (ve, v_sg[2]))
            q_pump(6)
            SE.wait_ge(pe.h, v_A)
            v_cev = ac.inc(SE.copy(comb[:, h * 512:(h + 1) * 512], pb[6 + par][:, :]))
            set_res(f"pbA{par}_rd", (ac, v_cev))
            last_cev[0] = v_cev

        for h in range(H):
            q_pump(6)
            emit_q(h)
            if h >= 1:
                emit_elu(h - 1)
            if h >= 2:
                emit_nc(h - 2)
            if h >= 3:
                emit_fin(h - 3)
        emit_elu(H - 1)
        emit_nc(H - 2)
        emit_fin(H - 3)
        emit_nc(H - 1)
        emit_fin(H - 2)
        emit_fin(H - 1)
        return last_cev[0]

    def emit_wo(s, v_comb, s1_guard=None):
        v_stats = None
        evs = {}
        for fc in range(C):
            par = fc % 2
            ws = con_slot("wbig")
            q_wait(TE)
            wait_res(TE, f"pbwo{par}_rd")
            wait_res(TE, f"pbq{par}_rd")
            if fc == 0:
                TE.wait_ge(ac.h, v_comb)
            ins = None
            for hc in range(C):
                ins = TE.matmul(pb[par][:, :], wbig[ws][:, hc * 128:(hc + 1) * 128],
                                combc(hc), start=(hc == 0), stop=(hc == C - 1))
            v_wo = pe.inc(ins)
            free_slot("wbig", pe, v_wo)
            q_pump(6)
            SE.wait_ge(pe.h, v_wo)
            if s1_guard is not None:
                SE.wait_ge(ve.h, s1_guard[fc])
            v_ev = ac.inc(SE.copy(s1c(fc), pb[par][:, :]))
            set_res(f"pbwo{par}_rd", (ac, v_ev))
            evs[fc] = (ac, v_ev)
            if fc >= 2:
                ln_stat_chunk(fc - 2, s1c(fc - 2), ready=evs[fc - 2])
        for fc in (C - 2, C - 1):
            v = ln_stat_chunk(fc, s1c(fc), ready=evs[fc])
            if v is not None:
                v_stats = v
        return v_stats

    def emit_gate(s, ready):
        TE.wait_ge(qg.h, qg.n)
        wait_res(TE, "pbwo0_rd")
        wait_res(TE, "pbwo1_rd")
        ins = None
        for kc in range(C):
            TE.wait_ge(ready[kc][0].h, ready[kc][1])
            ins = TE.matmul(pb[0][0:8, :], gw_sb[:, kc * 8:(kc + 1) * 8], actc(kc),
                            start=(kc == 0), stop=(kc == C - 1))
        v_g = pe.inc(ins)
        SE.wait_ge(pe.h, v_g)
        v_sco = ac.inc(SE.activation(scores[:], pb[0][0:8, :], AF.Sigmoid))
        TE.wait_ge(ac.h, v_sco)
        ins = None
        for tt in range(4):
            ins = TE.transpose(pb[1][:, tt * 8:(tt + 1) * 8],
                               scores[:, tt * 128:(tt + 1) * 128], ident[0:8, 0:8])
        v_t = pe.inc(ins)
        SE.wait_ge(pe.h, v_t)
        v_stm = ac.inc(SE.copy(stm[:], pb[1][:, 0:32]))
        VE.wait_ge(ac.h, v_stm)
        stm3 = stm[:].rearrange("p (a b) -> p a b", a=4)
        ve_sync(VE.reduce_max(m01[:, 0:4].unsqueeze(2), stm3, axis=AX.X))
        m0b = m01[:, 0:4].unsqueeze(2).to_broadcast([128, 4, 8])
        VE.tensor_tensor(mask0[:].rearrange("p (a b) -> p a b", a=4), stm3, m0b,
                         op=ALU.is_equal)
        VE.scalar_tensor_tensor(stm2[:], mask0[:], -1e30, stm[:],
                                op0=ALU.mult, op1=ALU.add)
        stm23 = stm2[:].rearrange("p (a b) -> p a b", a=4)
        ve_sync(VE.reduce_max(m01[:, 4:8].unsqueeze(2), stm23, axis=AX.X))
        m1b = m01[:, 4:8].unsqueeze(2).to_broadcast([128, 4, 8])
        VE.tensor_tensor(mask1[:].rearrange("p (a b) -> p a b", a=4), stm23, m1b,
                         op=ALU.is_equal)
        VE.tensor_tensor(scl[:].rearrange("p (a b) -> p a b", a=4),
                         mask0[:].rearrange("p (a b) -> p a b", a=4), m0b, op=ALU.mult)
        VE.tensor_tensor(mask1[:].rearrange("p (a b) -> p a b", a=4),
                         mask1[:].rearrange("p (a b) -> p a b", a=4), m1b, op=ALU.mult)
        v_scl = ve.inc(VE.tensor_tensor(scl[:], scl[:], mask1[:], op=ALU.add))
        TE.wait_ge(ve.h, v_scl)
        wait_res(TE, "pbn0_rd")
        ins = None
        for tt in range(4):
            ins = TE.transpose(pb[2][0:8, tt * 128:(tt + 1) * 128],
                               scl[:, tt * 8:(tt + 1) * 8], ident[:, :])
        v_bt = pe.inc(ins)
        SE.wait_ge(pe.h, v_bt)
        v_sfm = ac.inc(SE.copy(sfm[:], pb[2][0:8, :]))
        set_res("pbn0_rd", (ac, v_sfm))
        return v_sfm

    def emit_ffn(s, v_sfm):
        TE.wait_ge(ac.h, v_sfm)
        v_stats = None
        v_bse = {}

        def emit_sbcast(e):
            bp = e % 2
            wait_res(VE, "selrow_rd")
            v_sel = ve.inc(VE.tensor_scalar(
                selrow[:], iota_c[0:8, 0:1].to_broadcast([8, 128]), float(e),
                None, op0=ALU.is_equal))
            TE.wait_ge(ve.h, v_sel)
            wait_res(TE, "pb4g_rd")
            v_sb = pe.inc(TE.matmul(pb[4][:, :], selrow[:],
                                    sfm[:], start=True, stop=True))
            set_res("selrow_rd", (pe, v_sb))
            SE.wait_ge(pe.h, v_sb)
            wait_res(SE, f"bcs_rd{bp}")
            v_bse[bp] = ac.inc(SE.copy(bc_s[bp][:], pb[4][:, :]))
            set_res("pb4g_rd", (ac, v_bse[bp]))

        emit_sbcast(0)
        for e in range(NEX):
            ep = e % 2
            hb = ep * 8
            v_hm = {}
            PRD = GE if POOL_H_PROD else VE
            sem_prd = gp if POOL_H_PROD else ve
            for hc in range(HCN):
                par = hc % 2
                ws = con_slot("wbig")
                q_wait(TE)
                wait_res(TE, f"pbh1{par}_rd")
                wait_res(TE, f"pbh3{par}_rd")
                ins = None
                for kc in range(C):
                    TE.matmul(pb[par][:, :], wbig[ws][:, kc * 128:(kc + 1) * 128],
                              actc(kc), start=(kc == 0), stop=(kc == C - 1))
                    ins = TE.matmul(pb[2 + par][:, :],
                                    wbig[ws][:, D + kc * 128:D + (kc + 1) * 128],
                                    actc(kc), start=(kc == 0), stop=(kc == C - 1))
                v_h = pe.inc(ins)
                free_slot("wbig", pe, v_h)
                q_pump(6)
                if e < 8:
                    VE.wait_ge(pe.h, v_h)
                    VE.wait_ge(ac.h, v_bse[ep])
                    wait_res(VE, f"fscr{par}0_rd")
                    v_h1s = ve.inc(VE.tensor_tensor(scrf(par, 0), pb[par][:, :],
                                                    bc_s[ep][:], op=ALU.mult))
                    set_res(f"pbh1{par}_rd", (ve, v_h1s))
                    SE.wait_ge(ve.h, v_h1s)
                    wait_res(SE, f"fscr{par}1_rd")
                    v_silu = ac.inc(SE.activation(scrf(par, 1), scrf(par, 0), AF.Silu))
                    set_res(f"fscr{par}0_rd", (ac, v_silu))
                    wait_res(VE, f"fscr{par}2_rd")
                    v_h3s = ve.inc(VE.tensor_tensor(scrf(par, 2), pb[2 + par][:, :],
                                                    bc_s[ep][:], op=ALU.mult))
                    set_res(f"pbh3{par}_rd", (ve, v_h3s))
                    v_h3s_last = v_h3s
                    PRD.wait_ge(ac.h, v_silu)
                    if POOL_H_PROD:
                        PRD.wait_ge(ve.h, v_h3s)
                    wait_res(PRD, f"h_rd{ep}")
                    v_m = sem_prd.inc(PRD.tensor_tensor(combc(hb + hc), scrf(par, 1),
                                                        scrf(par, 2), op=ALU.mult))
                    set_res(f"fscr{par}1_rd", (sem_prd, v_m))
                    set_res(f"fscr{par}2_rd", (sem_prd, v_m))
                else:
                    SE.wait_ge(pe.h, v_h)
                    wait_res(SE, f"fscr{par}1_rd")
                    v_silu = ac.inc(SE.activation(scrf(par, 1), pb[par][:, :], AF.Silu))
                    set_res(f"pbh1{par}_rd", (ac, v_silu))
                    VE.wait_ge(ac.h, v_silu)
                    wait_res(VE, f"h_rd{ep}")
                    v_m = ve.inc(VE.tensor_tensor(combc(hb + hc), scrf(par, 1),
                                                  pb[2 + par][:, :], op=ALU.mult))
                    set_res(f"pbh3{par}_rd", (ve, v_m))
                    set_res(f"fscr{par}1_rd", (ve, v_m))
                v_hm[hc] = (sem_prd if e < 8 else ve, v_m)
            if e < 8:
                set_res(f"bcs_rd{ep}", (ve, v_h3s_last))
            if e + 1 < 8:
                emit_sbcast(e + 1)
            v_w2_last = None
            for fc in range(C):
                par2 = fc % 2
                w2s = con_slot("w2b")
                q_wait(TE)
                wait_res(TE, f"pbw2{par2}_rd")
                ins = None
                for kc in range(HCN):
                    if fc == 0:
                        sem_h, v_h_ = v_hm[kc]
                        TE.wait_ge(sem_h.h, v_h_)
                    ins = TE.matmul(pb[5 + par2][:, :], w2b[w2s][:, kc * 128:(kc + 1) * 128],
                                    combc(hb + kc), start=(kc == 0), stop=(kc == HCN - 1))
                v_w2 = pe.inc(ins)
                free_slot("w2b", pe, v_w2)
                q_pump(6)
                if e == 0:
                    SE.wait_ge(pe.h, v_w2)
                    v_ev = ac.inc(SE.copy(s1c(fc), pb[5 + par2][:, :]))
                    set_res(f"pbw2{par2}_rd", (ac, v_ev))
                    set_res(f"s1w_{fc}", (ac, v_ev))
                elif POOL_W2_EVAC and fc % 2 == 1:
                    SE.wait_ge(pe.h, v_w2)
                    wait_res(SE, "bcb_rd")
                    v_cp = ac.inc(SE.copy(bc_b[:], pb[5 + par2][:, :]))
                    set_res(f"pbw2{par2}_rd", (ac, v_cp))
                    GE.wait_ge(ac.h, v_cp)
                    wait_res(GE, f"s1w_{fc}")
                    v_ev = gp.inc(GE.tensor_tensor(s1c(fc), bc_b[:], s1c(fc), op=ALU.add))
                    set_res("bcb_rd", (gp, v_ev))
                    set_res(f"s1w_{fc}", (gp, v_ev))
                else:
                    VE.wait_ge(pe.h, v_w2)
                    wait_res(VE, f"s1w_{fc}")
                    v_ev = ve.inc(VE.scalar_tensor_tensor(s1c(fc), pb[5 + par2][:, :], 0.0,
                                                          s1c(fc), op0=ALU.add, op1=ALU.add))
                    set_res(f"pbw2{par2}_rd", (ve, v_ev))
                    set_res(f"s1w_{fc}", (ve, v_ev))
                if e == NEX - 1 and fc >= 2:
                    ln_stat_chunk(fc - 2, s1c(fc - 2), ready=res[f"s1w_{fc - 2}"][0])
                v_w2_last = v_w2
            if e == NEX - 1:
                for fc in (C - 2, C - 1):
                    v = ln_stat_chunk(fc, s1c(fc), ready=res[f"s1w_{fc}"][0])
                    if v is not None:
                        v_stats = v
            set_res(f"h_rd{ep}", (pe, v_w2_last))
        return v_stats

    def dump(name, src, dst=None):
        if not debug:
            return
        for sm in (pe, ve, ac, gp):
            SY.wait_ge(sm.h, sm.n)
        qo.inc(SY.dma_start(out=dbg[name] if dst is None else dst, in_=src), 16)
        v = qo.n
        for e in (TE, VE, SE):
            e.wait_ge(qo.h, v)

    out_v = out_h.rearrange("(c p) t -> p c t", p=128)

    def store_sub(s):
        qo.inc(SY.dma_start(out=out_v[:, 0:8, s * ST:(s + 1) * ST],
                            in_=x_sb_v[:, 0:8, s * ST:(s + 1) * ST]), 16)
        qo.inc(SY.dma_start(out=out_v[:, 8:16, s * ST:(s + 1) * ST],
                            in_=x_sb_v[:, 8:16, s * ST:(s + 1) * ST]), 16)

    pending = None
    for s in range(NS):
        for h in range(H):
            q_register(*mk_head_unit(h))
        for fc in range(C):
            q_register(*mk_wo_unit(fc))
        g13 = 0
        g2 = 0
        for e in range(NEX):
            for hc in range(HCN):
                q_register(*mk_w13_unit(e, hc, g13))
                g13 += 1
            for fc in range(C):
                q_register(*mk_w2_unit(e, fc, g2))
                g2 += 1
        q_pump(6)

        if s == 0:
            dump("dbg_x0", x_sb[:])
            dump("dbg_wa", wa_sb[:])
            dump("dbg_waT", waT_sb[:])
            dump("dbg_sm", sm_sb[:])
            dump("dbg_s4", small4[:])
        v1 = emit_ln_stats(lambda c: xc(s, c))
        ap1 = emit_ln_finish(s, lambda c: xc(s, c), "pre", v1,
                             bct=(bc_s[0], bc_s[1]) if pending is not None else None)
        if s == 0:
            dump("dbg_xn", act[:])
        v_comb = emit_attn(s, q0_ready=ap1)
        if s == 0:
            dump("dbg_comb", comb[:])
        guard = None
        if pending is not None:
            sp, v_bcev_p = pending
            guard = emit_ln_applies_deferred(sp, v_bcev_p)
            SY.wait_ge(ve.h, guard[-1])
            store_sub(sp)
            pending = None
        v2 = emit_wo(s, v_comb, s1_guard=guard)
        if s == 0:
            dump("dbg_attn", s1[:])
        ap2 = emit_ln_finish(s, s1c, "post", v2)
        if s == 0 and debug:
            dump("dbg_x1", x_sb[:].rearrange("p (c t) -> p c t", c=C)[:, :, 0:ST],
                 dst=dbg["dbg_x1"].rearrange("p (c t) -> p c t", c=C))
        v3 = emit_ln_stats(lambda c: xc(s, c), src_ready=ap2)
        ap3 = emit_ln_finish(s, lambda c: xc(s, c), "pre", v3)
        if s == 0:
            dump("dbg_y", act[:])
        v_sfm = emit_gate(s, ready=ap3)
        if s == 0:
            dump("dbg_scores", scores[:])
            dump("dbg_sfm", sfm[:])
            dump("dbg_stm", stm[:])
            dump("dbg_scl", scl[:])
            dump("dbg_m01", m01[:])
        v4 = emit_ffn(s, v_sfm)
        if s == 0:
            dump("dbg_ffn", s1[:])
        if s + 1 < NS:
            pending = (s, emit_ln_finish(s, s1c, "post", v4, defer=True))
        else:
            emit_ln_finish(s, s1c, "post", v4)
            SY.wait_ge(ve.h, ve.n)
            SY.wait_ge(gp.h, gp.n)
            store_sub(s)

    SY.wait_ge(qo.h, qo.n)
    return nc


def _swz2(w):
    a, b = w.shape
    return np.ascontiguousarray(
        w.reshape(a // 128, 128, b // 128, 128).transpose(2, 1, 0, 3)
    ).reshape(b // 128, 128, a)


def kernel(**inputs):
    f32 = np.float32
    x = np.asarray(inputs["L0_attn_hidden_states"], dtype=f32)
    Wq = np.asarray(inputs["L0_attn_Wq"], dtype=f32)
    Wo = np.asarray(inputs["L0_attn_Wo"], dtype=f32)
    bw = np.asarray(inputs["L0_attn_bank_weights"], dtype=f32)
    mem = np.asarray(inputs["L0_attn_memories"], dtype=f32)
    mnorm = np.asarray(inputs["L0_attn_memory_norms"], dtype=f32)
    gw = np.asarray(inputs["L0_ffn_gate_w"], dtype=f32)
    w1 = np.asarray(inputs["L0_ffn_w1"], dtype=f32)
    w2 = np.asarray(inputs["L0_ffn_w2"], dtype=f32)
    w3 = np.asarray(inputs["L0_ffn_w3"], dtype=f32)
    ws1 = np.asarray(inputs["L0_ffn_ws1"], dtype=f32)
    ws2 = np.asarray(inputs["L0_ffn_ws2"], dtype=f32)
    ws3 = np.asarray(inputs["L0_ffn_ws3"], dtype=f32)

    xf = np.ascontiguousarray(x.reshape(NCORES * TCORE, D))
    xT_cores = [np.ascontiguousarray(xf[i * TCORE:(i + 1) * TCORE].T) for i in range(NCORES)]

    wq_host = _swz2(np.ascontiguousarray(Wq.T))
    wo_host = _swz2(np.ascontiguousarray(Wo.T))

    def swz_e(w_e):
        return _swz2(np.ascontiguousarray(w_e.T))

    w1_host = np.stack([swz_e(w1[e]) for e in range(8)] + [swz_e(ws1)])
    w3_host = np.stack([swz_e(w3[e]) for e in range(8)] + [swz_e(ws3)])
    w2_host = np.stack([swz_e(w2[e]) for e in range(8)] + [swz_e(ws2)])

    gw_host = np.ascontiguousarray(
        gw.T.reshape(16, 128, 8).transpose(1, 0, 2)).reshape(128, 128)
    mh_host = np.ascontiguousarray(np.concatenate(
        [mem.transpose(2, 1, 0, 3).reshape(128, H, NB * 128),
         mnorm.transpose(2, 1, 0)], axis=2))
    bw_host = np.ascontiguousarray(bw)
    nf_host = np.ascontiguousarray(mnorm.reshape(NB, H * 128))

    nc = build_program()
    shared = {
        "wq_h": wq_host, "wo_h": wo_host, "mh_h": mh_host,
        "w1_h": w1_host, "w3_h": w3_host, "w2_h": w2_host,
        "gw_h": gw_host, "bw_h": bw_host, "nf_h": nf_host,
    }
    in_maps = [dict(shared, xT=xT_cores[i]) for i in range(NCORES)]
    res = run_bass_kernel_spmd(nc, in_maps, core_ids=list(range(NCORES)))
    out = np.concatenate([np.asarray(r["out"]).T for r in res.results], axis=0)
    return np.ascontiguousarray(out.reshape(4, 2048, 2048)).astype(np.float32)


# version 36
# speedup vs baseline: 1.0283x; 1.0260x over previous
from collections import deque
from contextlib import ExitStack

import numpy as np

import concourse.bass as bass
import concourse.mybir as mybir
from concourse.bass_utils import run_bass_kernel_spmd

F32 = mybir.dt.float32
BF16 = mybir.dt.bfloat16
AF = mybir.ActivationFunctionType
ALU = mybir.AluOpType
AX = mybir.AxisListType

NCORES = 8
D = 2048
C = 16
TCORE = 1024
ST = 512
NS = 2
H = 16
NB = 4
HCN = 8
NEX = 9
EPS_LN = 1e-5
EPS_ATT = 1e-6
INV_D = 1.0 / D

FENCE_SE = True
FENCE_GE = True
POOL_ELU_ADD = False
POOL_A_REDUCE = False
POOL_LN_APPLY = True
POOL_LN_CAST = True
POOL_H_PROD = True
POOL_W2_EVAC = False


class Sem:
    __slots__ = ("h", "n")

    def __init__(self, h):
        self.h = h
        self.n = 0

    def inc(self, instr, k=1):
        instr.then_inc(self.h, k)
        self.n += k
        return self.n


class AutoSem(Sem):

    def inc(self, instr, k=1):
        return self.n


_AUTO_OPS = {
    "tensor_tensor", "tensor_scalar", "tensor_scalar_mul", "tensor_scalar_min",
    "tensor_scalar_max", "tensor_scalar_add", "scalar_tensor_tensor",
    "tensor_copy", "tensor_add", "tensor_sub", "tensor_mul", "memset",
    "reduce_max", "reduce_sum", "tensor_reduce", "reciprocal", "max",
    "activation", "copy", "square", "iota", "select",
}


class _Counting:

    def __init__(self, eng, sem):
        self._eng = eng
        self._sem = sem

    def __getattr__(self, name):
        attr = getattr(self._eng, name)
        if name in _AUTO_OPS and callable(attr):
            sem = self._sem

            def call(*a, **k):
                ins = attr(*a, **k)
                ins.then_inc(sem.h, 1)
                sem.n += 1
                return ins
            return call
        return attr


def _ap_ranges(args, kwargs):
    aps = []
    for x in args:
        if hasattr(x, "tensor") and hasattr(x, "offset"):
            aps.append(x)
    for x in kwargs.values():
        if hasattr(x, "tensor") and hasattr(x, "offset"):
            aps.append(x)

    def ext(ap):
        try:
            span = 1
            for step, count in ap.ap:
                span += abs(int(step)) * (int(count) - 1)
            off = int(ap.offset)
            return (ap.tensor.name, off, off + span)
        except Exception:
            return (ap.tensor.name, 0, 1 << 60)
    return ([ext(aps[0])] if aps else []), [ext(x) for x in aps[1:]]


def _overlap(r1, r2):
    for n1, s1, e1 in r1:
        for n2, s2, e2 in r2:
            if n1 == n2 and s1 < e2 and s2 < e1:
                return True
    return False


class AutoFenced:

    def __init__(self, eng, sem):
        self._eng = eng
        self._sem = sem
        self._recent = []
        self._covered = 0

    def __getattr__(self, name):
        attr = getattr(self._eng, name)
        if name in _AUTO_OPS and callable(attr):
            eng, sem = self._eng, self._sem

            def call(*a, **k):
                w, r = _ap_ranges(a, k)
                need = 0
                for val, pw, pr in self._recent:
                    if _overlap(r, pw) or _overlap(w, pw) or _overlap(w, pr):
                        need = max(need, val)
                if need > self._covered:
                    eng.wait_ge(sem.h, need)
                    self._covered = need
                    self._recent = [t for t in self._recent if t[0] > need]
                ins = attr(*a, **k)
                ins.then_inc(sem.h, 1)
                sem.n += 1
                self._recent.append((sem.n, w, r))
                if len(self._recent) > 24:
                    v = self._recent[0][0]
                    eng.wait_ge(sem.h, v)
                    self._covered = v
                    self._recent = [t for t in self._recent if t[0] > v]
                return ins
            return call
        return attr

    def wait_covered(self, val):
        if val > self._covered:
            self._covered = val
            self._recent = [t for t in self._recent if t[0] > val]


def build_program(debug=False):
    nc = bass.Bass("TRN2", target_bir_lowering=False)
    stack = ExitStack()
    nc._keepalive_stack = stack

    def sb(name, shape, dt):
        return stack.enter_context(nc.sbuf_tensor(name, shape, dt))

    xT = nc.dram_tensor("xT", [D, TCORE], F32, kind="ExternalInput").ap()
    wq_h = nc.dram_tensor("wq_h", [H, 128, D], F32, kind="ExternalInput").ap()
    wo_h = nc.dram_tensor("wo_h", [C, 128, D], F32, kind="ExternalInput").ap()
    mh_h = nc.dram_tensor("mh_h", [128, H, 516], F32, kind="ExternalInput").ap()
    w1_h = nc.dram_tensor("w1_h", [NEX, HCN, 128, D], F32, kind="ExternalInput").ap()
    w3_h = nc.dram_tensor("w3_h", [NEX, HCN, 128, D], F32, kind="ExternalInput").ap()
    w2_h = nc.dram_tensor("w2_h", [NEX, C, 128, 1024], F32, kind="ExternalInput").ap()
    gw_h = nc.dram_tensor("gw_h", [128, 128], F32, kind="ExternalInput").ap()
    bw_h = nc.dram_tensor("bw_h", [H, NB], F32, kind="ExternalInput").ap()
    nf_h = nc.dram_tensor("nf_h", [NB, 2048], F32, kind="ExternalInput").ap()
    out_h = nc.dram_tensor("out", [D, TCORE], F32, kind="ExternalOutput").ap()
    dbg = {}
    if debug:
        for nm, shape, dt in [
            ("dbg_x0", [128, C * TCORE], F32), ("dbg_xn", [128, C * ST], BF16),
            ("dbg_comb", [128, C * ST], BF16), ("dbg_attn", [128, C * ST], F32),
            ("dbg_x1", [128, C * ST], F32), ("dbg_y", [128, C * ST], BF16),
            ("dbg_scores", [8, ST], F32), ("dbg_sfm", [8, ST], F32),
            ("dbg_ffn", [128, C * ST], F32), ("dbg_wa", [H, NB], F32),
            ("dbg_waT", [NB, H], F32), ("dbg_sm", [H, NB], F32),
            ("dbg_s4", [NB, 4], F32), ("dbg_stm", [128, 32], F32),
            ("dbg_scl", [128, 32], F32), ("dbg_m01", [128, 8], F32),
        ]:
            dbg[nm] = nc.dram_tensor(nm, shape, dt, kind="ExternalOutput").ap()

    x_sb = sb("x_sb", [128, C * TCORE], F32)
    s1 = sb("s1", [128, C * ST], F32)
    act = sb("act", [128, C * ST], BF16)
    comb = sb("comb", [128, C * ST], BF16)
    NWB = 3
    wbig = [sb(f"wbig{i}", [128, 2 * D], BF16) for i in range(NWB)]
    w2b = [sb(f"w2b{i}", [128, 1024], BF16) for i in range(3)]
    mhb = [sb(f"mhb{i}", [128, 516], BF16) for i in range(6)]
    sig = [sb(f"sig{i}", [128, ST], BF16) for i in range(3)]
    scrF = sb("scrF", [128, 6 * ST], F32)

    def scrf(par, i):
        return scrF[:, (par * 3 + i) * ST:(par * 3 + i + 1) * ST]

    scrA = [sb(f"scrA{i}", [128, ST], F32) for i in range(2)]
    sgs = [sb(f"sgs{i}", [128, ST], BF16) for i in range(2)]
    sel4 = sb("sel4", [NB, NB * 128], BF16)
    c4b = [sb(f"c4b{i}", [NB, ST], BF16) for i in range(2)]
    scrN = sb("scrN", [NB, 2 * ST], F32)

    def scrNv(par):
        return scrN[:, par * ST:(par + 1) * ST]

    bc_a = sb("bc_a", [128, ST], F32)
    bc_b = sb("bc_b", [128, ST], F32)
    bc_s = [sb(f"bc_s{i}", [128, ST], F32) for i in range(2)]
    gw_sb = sb("gw_sb", [128, 128], BF16)
    bw_sb = sb("bw_sb", [H, NB], F32)
    sm_sb = sb("sm_sb", [H, NB], F32)
    wa_sb = sb("wa_sb", [H, NB], F32)
    waT_sb = sb("waT_sb", [NB, H], F32)
    small16 = sb("small16", [H, 8], F32)
    small4 = sb("small4", [NB, 4], F32)
    act_row = sb("act_row", [1, 4], F32)
    nf_sb = scrF[0:4, 0:2048]
    scores = sb("scores", [8, ST], F32)
    sfm = sb("sfm", [8, ST], F32)
    stm = sb("stm", [128, 32], F32)
    mask0 = sb("mask0", [128, 32], F32)
    stm2 = sb("stm2", [128, 32], F32)
    mask1 = sb("mask1", [128, 32], F32)
    scl = sb("scl", [128, 32], F32)
    m01 = sb("m01", [128, 8], F32)
    selrow = sb("selrow", [8, 128], F32)
    ident = sb("ident", [128, 128], F32)
    iota_r = sb("iota_r", [128, 128], F32)
    iota_c = sb("iota_c", [128, 1], F32)
    ones_col_bf = sb("ones_col_bf", [128, 1], BF16)
    ones_col_f = sb("ones_col_f", [128, 1], F32)
    ones_row_f = sb("ones_row_f", [1, 128], F32)
    consts = sb("consts", [128, 8], F32)

    pb = [stack.enter_context(nc.psum_tensor(f"pb{i}", [128, 512], F32)) for i in range(8)]

    qx = Sem(stack.enter_context(nc.semaphore("qx")))
    qs = Sem(stack.enter_context(nc.semaphore("qs")))
    qg = Sem(stack.enter_context(nc.semaphore("qg")))
    qwp = [Sem(stack.enter_context(nc.semaphore(f"qw{i}"))) for i in range(16)]
    qo = Sem(stack.enter_context(nc.semaphore("qo")))
    pe = Sem(stack.enter_context(nc.semaphore("pe")))
    ve = AutoSem(stack.enter_context(nc.semaphore("ve")))
    ac = AutoSem(stack.enter_context(nc.semaphore("ac")))
    gp = AutoSem(stack.enter_context(nc.semaphore("gp")))

    TE = nc.tensor
    VE = AutoFenced(nc.vector, ve)
    SE = AutoFenced(nc.scalar, ac) if FENCE_SE else _Counting(nc.scalar, ac)
    GE = AutoFenced(nc.gpsimd, gp) if FENCE_GE else _Counting(nc.gpsimd, gp)
    SY = nc.sync

    def join(*engs):
        for e in engs:
            for s in (pe, ve, ac, gp):
                e.wait_ge(s.h, s.n)

    def ve_sync(instr):
        return ve.n

    def xc(s, c):
        return x_sb[:, c * TCORE + s * ST: c * TCORE + s * ST + ST]

    def actc(c):
        return act[:, c * ST:(c + 1) * ST]

    def combc(c):
        return comb[:, c * ST:(c + 1) * ST]

    def s1c(c):
        return s1[:, c * ST:(c + 1) * ST]

    r_su = scrA[0][0:1, :]
    r_msq = scrA[0][0:1, :]
    r_sq = scrA[1][0:1, :]
    r_mean = scrN[0:1, 0:ST]
    r_var = scrN[0:1, ST:2 * ST]
    r_rstd = bc_s[0][0:1, :]
    r_b = sfm[0:1, :]

    res = {}

    def wait_res(eng, key):
        for s, v in res.get(key, ()):
            eng.wait_ge(s.h, v)

    def set_res(key, *pairs):
        res[key] = list(pairs)

    NBUF = {"wbig": NWB, "mhb": 6, "w2b": 3}
    wreg = {k: 0 for k in NBUF}
    wcon = {k: 0 for k in NBUF}
    free_events = {k: [] for k in NBUF}

    def reg_slot(cls):
        s_ = wreg[cls] % NBUF[cls]
        wreg[cls] += 1
        return s_

    def con_slot(cls):
        s_ = wcon[cls] % NBUF[cls]
        wcon[cls] += 1
        return s_

    def free_slot(cls, sem, val):
        free_events[cls].append((sem, val))

    def slot_wait(cls, seq):
        if seq >= NBUF[cls]:
            sem, val = free_events[cls][seq - NBUF[cls]]
            GE.wait_ge(sem.h, val)

    def slot_ready(cls, seq):
        return seq < NBUF[cls] or len(free_events[cls]) > seq - NBUF[cls]

    units = []
    classQ = {c: [] for c in NBUF}
    cls_issued = {c: 0 for c in NBUF}
    cls_consumed = {c: 0 for c in NBUF}
    consumed = [0]
    issue_seq = [0]

    def q_register(cls, ready_fn, issue_fn):
        units.append((cls, len(classQ[cls])))
        classQ[cls].append([ready_fn, issue_fn, None])

    def q_pump(k):
        for cls in NBUF:
            q = classQ[cls]
            while (cls_issued[cls] < len(q)
                   and cls_issued[cls] < cls_consumed[cls] + k
                   and q[cls_issued[cls]][0]()):
                rec = q[cls_issued[cls]]
                sem = qwp[issue_seq[0] % 16]
                issue_seq[0] += 1
                GE.wait_ge(sem.h, sem.n)
                rec[1](sem)
                rec[2] = (sem, sem.n)
                cls_issued[cls] += 1

    def q_wait(eng):
        cls, seq = units[consumed[0]]
        consumed[0] += 1
        cls_consumed[cls] += 1
        while classQ[cls][seq][2] is None:
            q_pump(3)
        sem, v = classQ[cls][seq][2]
        eng.wait_ge(sem.h, v)

    def mk_head_unit(h):
        wseq = wreg["wbig"]
        mseq = wreg["mhb"]
        ws = reg_slot("wbig")
        ms = reg_slot("mhb")

        def ready():
            return slot_ready("wbig", wseq) and slot_ready("mhb", mseq)

        def fn(sem):
            slot_wait("wbig", wseq)
            sem.inc(GE.dma_start(out=wbig[ws][:, 0:D], in_=wq_h[h]), 16)
            slot_wait("mhb", mseq)
            sem.inc(GE.dma_start(out=mhb[ms][:], in_=mh_h[:, h, :]), 16)
        return "wbig", ready, fn

    def mk_wo_unit(fc):
        wseq = wreg["wbig"]
        ws = reg_slot("wbig")

        def ready():
            return slot_ready("wbig", wseq)

        def fn(sem):
            slot_wait("wbig", wseq)
            sem.inc(GE.dma_start(out=wbig[ws][:, 0:D], in_=wo_h[fc]), 16)
        return "wbig", ready, fn

    def mk_w13_unit(e, hc, g):
        wseq = wreg["wbig"]
        ws = reg_slot("wbig")

        def ready():
            return slot_ready("wbig", wseq)

        def fn(sem):
            slot_wait("wbig", wseq)
            sem.inc(GE.dma_start(out=wbig[ws][:, 0:D], in_=w1_h[e, hc]), 16)
            sem.inc(GE.dma_start(out=wbig[ws][:, D:2 * D], in_=w3_h[e, hc]), 16)
        return "wbig", ready, fn

    def mk_w2_unit(e, fc, g):
        w2seq = wreg["w2b"]
        w2s = reg_slot("w2b")

        def ready():
            return slot_ready("w2b", w2seq)

        def fn(sem):
            slot_wait("w2b", w2seq)
            sem.inc(GE.dma_start(out=w2b[w2s][:], in_=w2_h[e, fc]), 16)
        return "w2b", ready, fn

    xT_v = xT.rearrange("(c p) t -> p c t", p=128)
    x_sb_v = x_sb[:].rearrange("p (c t) -> p c t", c=C)
    for g in range(4):
        qx.inc(SY.dma_start(out=x_sb_v[:, 4 * g:4 * g + 4, :],
                            in_=xT_v[:, 4 * g:4 * g + 4, :]), 16)
    v_x = qx.n
    qs.inc(SY.dma_start(out=bw_sb[:], in_=bw_h), 16)
    qs.inc(SY.dma_start(out=nf_sb[:], in_=nf_h), 16)
    qg.inc(GE.dma_start(out=gw_sb[:], in_=gw_h), 16)
    v_small = qs.n

    GE.iota(iota_r[:], pattern=[[1, 128]], base=0, channel_multiplier=0,
            allow_small_or_imprecise_dtypes=True)
    v_iota = gp.inc(GE.iota(iota_c[:], pattern=[[0, 1]], base=0, channel_multiplier=1,
                            allow_small_or_imprecise_dtypes=True))

    VE.memset(ones_col_bf[:], 1.0)
    VE.memset(ones_col_f[:], 1.0)
    VE.memset(ones_row_f[:], 1.0)
    VE.memset(consts[:, 0:1], EPS_LN)
    VE.memset(consts[:, 1:2], -0.5)
    VE.memset(consts[:, 2:3], -1.0)
    VE.wait_ge(gp.h, v_iota)
    VE.tensor_scalar(ident[:], iota_r[:], iota_c[:, 0:1], None, op0=ALU.is_equal)
    for n in range(NB):
        VE.tensor_scalar(sel4[:, n * 128:(n + 1) * 128],
                         iota_c[0:NB, 0:1].to_broadcast([NB, 128]), float(n), None,
                         op0=ALU.is_equal)

    VE.wait_ge(qs.h, v_small)
    ve_sync(VE.reduce_sum(small4[:, 0:1], nf_sb[:], axis=AX.X))
    VE.tensor_scalar(small4[:, 1:2], small4[:, 0:1], EPS_ATT, None, op0=ALU.is_ge)
    VE.wait_ge(qs.h, v_small)
    ve_sync(VE.reduce_max(small16[:, 0:1], bw_sb[:], axis=AX.X))
    v2 = ve.inc(VE.tensor_tensor(sm_sb[:], bw_sb[:],
                                 small16[:, 0:1].to_broadcast([H, NB]), op=ALU.subtract))
    SE.wait_ge(ve.h, v2)
    v3 = ac.inc(SE.activation(sm_sb[:], sm_sb[:], AF.Exp))
    VE.wait_ge(ac.h, v3)
    ve_sync(VE.reduce_sum(small16[:, 1:2], sm_sb[:], axis=AX.X))
    ve_sync(VE.reciprocal(small16[:, 2:3], small16[:, 1:2]))
    v4 = ve.inc(VE.tensor_tensor(sm_sb[:], sm_sb[:],
                                 small16[:, 2:3].to_broadcast([H, NB]), op=ALU.mult))
    TE.wait_ge(ve.h, v4)
    v5 = pe.inc(TE.transpose(pb[6][0:1, 0:4], small4[0:4, 1:2], ident[0:4, 0:4]))
    SE.wait_ge(pe.h, v5)
    v6 = ac.inc(SE.copy(act_row[:], pb[6][0:1, 0:4]))
    TE.wait_ge(ac.h, v6)
    v7 = pe.inc(TE.matmul(pb[7][0:16, 0:4], ones_row_f[0:1, 0:16], act_row[:],
                          start=True, stop=True))
    VE.wait_ge(pe.h, v7)
    v8 = ve.inc(VE.tensor_tensor(wa_sb[:], pb[7][0:16, 0:4], sm_sb[:], op=ALU.mult))
    TE.wait_ge(ve.h, v8)
    v9 = pe.inc(TE.transpose(pb[6][0:4, 0:16], wa_sb[:], ident[0:16, 0:16]))
    SE.wait_ge(pe.h, v9)
    ac.inc(SE.copy(waT_sb[:], pb[6][0:4, 0:16]))

    def ln_stat_chunk(c, src_ap, ready=None):
        if c == 0:
            TE.wait_ge(qx.h, v_x)
            wait_res(TE, "pbn0_rd")
            wait_res(TE, "pbn1_rd")
        if ready is not None:
            TE.wait_ge(ready[0].h, ready[1])
            SE.wait_ge(ready[0].h, ready[1])
        TE.matmul(pb[2][0:1, :], ones_col_f[:], src_ap,
                  start=(c == 0), stop=(c == C - 1))
        v_sq = ac.inc(SE.activation(actc(c), src_ap, AF.Square))
        TE.wait_ge(ac.h, v_sq)
        ins = TE.matmul(pb[3][0:1, :], ones_col_bf[:], actc(c),
                        start=(c == 0), stop=(c == C - 1))
        return pe.inc(ins) if c == C - 1 else None

    def emit_ln_stats(src_fn, src_ready=None):
        v_stats = None
        for c in range(C):
            r = src_ready[c] if src_ready is not None else None
            v = ln_stat_chunk(c, src_fn(c), ready=r)
            if v is not None:
                v_stats = v
        return v_stats

    def emit_ln_finish(s, src_fn, mode, v_stats, bct=None, defer=False):
        bca_t, bcb_t = bct if bct is not None else (bc_a, bc_b)
        SE.wait_ge(pe.h, v_stats)
        SE.copy(r_su, pb[2][0:1, :])
        v_ev = ac.inc(SE.copy(r_sq, pb[3][0:1, :]))
        TE.wait_ge(ac.h, v_ev)
        TE.matmul(pb[2][0:1, :], sel4[0:4, 0:1], sel4[0:4, :], start=True, stop=True)
        set_res("pbn0_rd", (ac, v_ev))
        set_res("pbn1_rd", (ac, v_ev))
        VE.wait_ge(ac.h, v_ev)
        ve_sync(VE.tensor_scalar_mul(r_mean, r_su, INV_D))
        VE.tensor_tensor(r_msq, r_mean, r_mean, op=ALU.mult)
        v_var = ve.inc(VE.scalar_tensor_tensor(r_var, r_sq, INV_D, r_msq,
                                               op0=ALU.mult, op1=ALU.subtract))
        TE.wait_ge(ve.h, v_var)
        TE.matmul(pb[2][0:1, :], sel4[0:4, 0:1], sel4[0:4, :], start=True, stop=True)
        SE.wait_ge(ve.h, v_var)
        SE.activation(r_var, r_var, AF.Ln, bias=consts[0:1, 0:1])
        v_rstd = ac.inc(SE.activation(r_rstd, r_var, AF.Exp, scale=consts[0:1, 1:2]))
        TE.wait_ge(ac.h, v_rstd)
        TE.matmul(pb[2][0:1, :], sel4[0:4, 0:1], sel4[0:4, :], start=True, stop=True)
        VE.wait_ge(ac.h, v_rstd)
        v_b = ve.inc(VE.scalar_tensor_tensor(r_b, r_mean, -1.0, r_rstd,
                                             op0=ALU.mult, op1=ALU.mult))
        TE.wait_ge(ve.h, v_b)
        for key in ("pbbc_rd", "pbcb0_rd", "pbcb1_rd", "pb4g_rd", "pbw20_rd"):
            wait_res(TE, key)
        TE.matmul(pb[4][:, :], ones_row_f[:], r_rstd, start=True, stop=True)
        v_bc = pe.inc(TE.matmul(pb[5][:, :], ones_row_f[:], r_b, start=True, stop=True))
        SE.wait_ge(pe.h, v_bc)
        if bct is not None:
            wait_res(SE, "bcs_rd0")
            wait_res(SE, "bcs_rd1")
        SE.copy(bca_t[:], pb[4][:, :])
        v_bcev = ac.inc(SE.copy(bcb_t[:], pb[5][:, :]))
        set_res("pbbc_rd", (ac, v_bcev))
        if defer:
            return v_bcev
        VE.wait_ge(ac.h, v_bcev)
        applies = []
        for c in range(C):
            VE.tensor_tensor(scrf(0, 0), src_fn(c), bca_t[:], op=ALU.mult)
            if mode == "pre":
                ins = VE.tensor_tensor(actc(c), scrf(0, 0), bcb_t[:], op=ALU.add)
            else:
                VE.tensor_tensor(scrf(0, 1), scrf(0, 0), bcb_t[:], op=ALU.add)
                ins = VE.tensor_tensor(xc(s, c), scrf(0, 1), xc(s, c), op=ALU.add)
            applies.append((ve, ve.inc(ins)))
        if bct is not None:
            set_res("bcs_rd0", (ve, applies[-1][1]))
            set_res("bcs_rd1", (ve, applies[-1][1]))
        return applies

    def emit_ln_applies_deferred(s, v_bcev):
        wait_res(VE, "scrf00_rd")
        wait_res(VE, "scrf012_rd")
        VE.wait_ge(ac.h, v_bcev)
        aps = []
        for c in range(C):
            VE.tensor_tensor(scrf(0, 0), s1c(c), bc_a[:], op=ALU.mult)
            VE.tensor_tensor(scrf(0, 1), scrf(0, 0), bc_b[:], op=ALU.add)
            aps.append(ve.inc(VE.tensor_tensor(xc(s, c), scrf(0, 1), xc(s, c),
                                               op=ALU.add)))
        set_res("pbbc_rd", (ve, aps[-1]))
        set_res("scrf00_rd", (ve, aps[-1]))
        set_res("scrf012_rd", (ve, aps[-1]))
        return aps

    def emit_attn(s, q0_ready):
        qstate = {}
        estate = {}
        nstate = {}
        last_cev = [0]

        def emit_q(h):
            par = h % 2
            ws = con_slot("wbig")
            ms = con_slot("mhb")
            q_wait(TE)
            wait_res(TE, f"pbq{par}_rd")
            ins = None
            for kc in range(C):
                if h == 0:
                    TE.wait_ge(q0_ready[kc][0].h, q0_ready[kc][1])
                ins = TE.matmul(pb[par][:, :], wbig[ws][:, kc * 128:(kc + 1) * 128],
                                actc(kc), start=(kc == 0), stop=(kc == C - 1))
            v_q = pe.inc(ins)
            free_slot("wbig", pe, v_q)
            qstate[h] = (v_q, ms)

        def emit_elu(h):
            par = h % 2
            sg = h % 3
            v_q, ms = qstate.pop(h)
            VE.wait_ge(pe.h, v_q)
            wait_res(VE, f"scrf{par}0_rd")
            v_min = ve.inc(VE.tensor_scalar_min(scrf(par, 0), pb[par][:, :], 0.0))
            SE.wait_ge(ve.h, v_min)
            wait_res(SE, f"scrf{par}12_rd")
            v_exp = ac.inc(SE.activation(scrf(par, 1), scrf(par, 0), AF.Exp))
            v_relu = ac.inc(SE.activation(scrf(par, 2), pb[par][:, :], AF.Relu))
            set_res(f"pbq{par}_rd", (ve, v_min), (ac, v_relu))
            set_res(f"scrf{par}0_rd", (ac, v_exp))
            EADD = GE if POOL_ELU_ADD else VE
            sem_eadd = gp if POOL_ELU_ADD else ve
            EADD.wait_ge(ac.h, v_relu)
            wait_res(EADD, f"sig{sg}_rd")
            v_sig = sem_eadd.inc(EADD.tensor_tensor(sig[sg][:], scrf(par, 1),
                                                    scrf(par, 2), op=ALU.add))
            set_res(f"scrf{par}12_rd", (sem_eadd, v_sig))
            estate[h] = (ms, (sem_eadd, v_sig))

        def emit_nc(h):
            par = h % 2
            sg = h % 3
            ms, (sem_sig, v_sig) = estate.pop(h)
            TE.wait_ge(sem_sig.h, v_sig)
            wait_res(TE, f"pbn{par}_rd")
            v_n = pe.inc(TE.matmul(pb[2 + par][0:4, :], mhb[ms][:, 512:516],
                                   sig[sg][:], start=True, stop=True))
            VE.wait_ge(pe.h, v_n)
            wait_res(VE, f"scrN{par}_rd")
            v_nmax = ve.inc(VE.tensor_scalar_max(scrNv(par), pb[2 + par][0:4, :], EPS_ATT))
            SE.wait_ge(ve.h, v_nmax)
            SE.activation(scrNv(par), scrNv(par), AF.Ln)
            v_cexp = ac.inc(SE.activation(scrNv(par), scrNv(par), AF.Exp,
                                          scale=consts[0:NB, 2:3]))
            set_res(f"scrN{par}_rd", (ac, v_cexp))
            VE.wait_ge(ac.h, v_cexp)
            wait_res(VE, f"c4_{par}_rd")
            v_c4 = ve.inc(VE.tensor_tensor(c4b[par][:], scrNv(par),
                                           waT_sb[:, h:h + 1].to_broadcast([NB, ST]),
                                           op=ALU.mult))
            nstate[h] = (ms, v_c4, v_nmax, (sem_sig, v_sig))

        def emit_fin(h):
            par = h % 2
            sg = h % 3
            ms, v_c4, v_nmax, (sem_sig, v_sig) = nstate.pop(h)
            banks = [2 + par, 4, 5, 2 + par]
            v_cb = [None] * NB
            TE.wait_ge(ve.h, v_c4)
            wait_res(TE, "pbbc_rd")
            for n in range(3):
                if n > 0:
                    wait_res(TE, f"pbcb{n - 1}_rd")
                v_cb[n] = pe.inc(TE.matmul(pb[banks[n]][:, :],
                                           sel4[:, n * 128:(n + 1) * 128],
                                           c4b[par][:], start=True, stop=True))
            v_sg = [None] * NB
            v_A = None
            for n in range(NB):
                VE.wait_ge(pe.h, v_cb[n])
                if n == 0:
                    VE.wait_ge(sem_sig.h, v_sig)
                wait_res(VE, f"sgs{n % 2}_rd")
                v_sg[n] = ve.inc(VE.tensor_tensor(sgs[n % 2][:], sig[sg][:],
                                                  pb[banks[n]][:, :], op=ALU.mult))
                if n == 0:
                    TE.wait_ge(ve.h, v_sg[0])
                    v_cb[3] = pe.inc(TE.matmul(pb[banks[3]][:, :],
                                               sel4[:, 3 * 128:4 * 128],
                                               c4b[par][:], start=True, stop=True))
                TE.wait_ge(ve.h, v_sg[n])
                if n == 0:
                    wait_res(TE, f"pbA{par}_rd")
                v_A = pe.inc(TE.matmul(pb[6 + par][:, :], mhb[ms][:, n * 128:(n + 1) * 128],
                                       sgs[n % 2][:], start=(n == 0), stop=(n == NB - 1)))
                set_res(f"sgs{n % 2}_rd", (pe, v_A))
            free_slot("mhb", pe, v_A)
            set_res(f"sig{sg}_rd", (ve, v_sg[3]))
            set_res(f"c4_{par}_rd", (pe, v_cb[3]))
            set_res(f"pbn{par}_rd", (ve, v_sg[3]))
            set_res("pbcb0_rd", (ve, v_sg[1]))
            set_res("pbcb1_rd", (ve, v_sg[2]))
            q_pump(6)
            SE.wait_ge(pe.h, v_A)
            v_cev = ac.inc(SE.copy(comb[:, h * 512:(h + 1) * 512], pb[6 + par][:, :]))
            set_res(f"pbA{par}_rd", (ac, v_cev))
            last_cev[0] = v_cev

        for h in range(H):
            q_pump(6)
            emit_q(h)
            if h >= 1:
                emit_elu(h - 1)
            if h >= 2:
                emit_nc(h - 2)
            if h >= 3:
                emit_fin(h - 3)
        emit_elu(H - 1)
        emit_nc(H - 2)
        emit_fin(H - 3)
        emit_nc(H - 1)
        emit_fin(H - 2)
        emit_fin(H - 1)
        return last_cev[0]

    def emit_wo(s, v_comb, s1_guard=None):
        v_stats = None
        evs = {}
        for fc in range(C):
            par = fc % 2
            ws = con_slot("wbig")
            q_wait(TE)
            wait_res(TE, f"pbwo{par}_rd")
            wait_res(TE, f"pbq{par}_rd")
            if fc == 0:
                TE.wait_ge(ac.h, v_comb)
            ins = None
            for hc in range(C):
                ins = TE.matmul(pb[par][:, :], wbig[ws][:, hc * 128:(hc + 1) * 128],
                                combc(hc), start=(hc == 0), stop=(hc == C - 1))
            v_wo = pe.inc(ins)
            free_slot("wbig", pe, v_wo)
            q_pump(6)
            SE.wait_ge(pe.h, v_wo)
            if s1_guard is not None:
                SE.wait_ge(ve.h, s1_guard[fc])
            v_ev = ac.inc(SE.copy(s1c(fc), pb[par][:, :]))
            set_res(f"pbwo{par}_rd", (ac, v_ev))
            evs[fc] = (ac, v_ev)
            if fc >= 2:
                ln_stat_chunk(fc - 2, s1c(fc - 2), ready=evs[fc - 2])
        for fc in (C - 2, C - 1):
            v = ln_stat_chunk(fc, s1c(fc), ready=evs[fc])
            if v is not None:
                v_stats = v
        return v_stats

    def emit_gate(s, ready):
        TE.wait_ge(qg.h, qg.n)
        wait_res(TE, "pbwo0_rd")
        wait_res(TE, "pbwo1_rd")
        ins = None
        for kc in range(C):
            TE.wait_ge(ready[kc][0].h, ready[kc][1])
            ins = TE.matmul(pb[0][0:8, :], gw_sb[:, kc * 8:(kc + 1) * 8], actc(kc),
                            start=(kc == 0), stop=(kc == C - 1))
        v_g = pe.inc(ins)
        SE.wait_ge(pe.h, v_g)
        v_sco = ac.inc(SE.activation(scores[:], pb[0][0:8, :], AF.Sigmoid))
        TE.wait_ge(ac.h, v_sco)
        ins = None
        for tt in range(4):
            ins = TE.transpose(pb[1][:, tt * 8:(tt + 1) * 8],
                               scores[:, tt * 128:(tt + 1) * 128], ident[0:8, 0:8])
        v_t = pe.inc(ins)
        SE.wait_ge(pe.h, v_t)
        v_stm = ac.inc(SE.copy(stm[:], pb[1][:, 0:32]))
        VE.wait_ge(ac.h, v_stm)
        stm3 = stm[:].rearrange("p (a b) -> p a b", a=4)
        ve_sync(VE.reduce_max(m01[:, 0:4].unsqueeze(2), stm3, axis=AX.X))
        m0b = m01[:, 0:4].unsqueeze(2).to_broadcast([128, 4, 8])
        VE.tensor_tensor(mask0[:].rearrange("p (a b) -> p a b", a=4), stm3, m0b,
                         op=ALU.is_equal)
        VE.scalar_tensor_tensor(stm2[:], mask0[:], -1e30, stm[:],
                                op0=ALU.mult, op1=ALU.add)
        stm23 = stm2[:].rearrange("p (a b) -> p a b", a=4)
        ve_sync(VE.reduce_max(m01[:, 4:8].unsqueeze(2), stm23, axis=AX.X))
        m1b = m01[:, 4:8].unsqueeze(2).to_broadcast([128, 4, 8])
        VE.tensor_tensor(mask1[:].rearrange("p (a b) -> p a b", a=4), stm23, m1b,
                         op=ALU.is_equal)
        VE.tensor_tensor(scl[:].rearrange("p (a b) -> p a b", a=4),
                         mask0[:].rearrange("p (a b) -> p a b", a=4), m0b, op=ALU.mult)
        VE.tensor_tensor(mask1[:].rearrange("p (a b) -> p a b", a=4),
                         mask1[:].rearrange("p (a b) -> p a b", a=4), m1b, op=ALU.mult)
        v_scl = ve.inc(VE.tensor_tensor(scl[:], scl[:], mask1[:], op=ALU.add))
        TE.wait_ge(ve.h, v_scl)
        wait_res(TE, "pbn0_rd")
        ins = None
        for tt in range(4):
            ins = TE.transpose(pb[2][0:8, tt * 128:(tt + 1) * 128],
                               scl[:, tt * 8:(tt + 1) * 8], ident[:, :])
        v_bt = pe.inc(ins)
        SE.wait_ge(pe.h, v_bt)
        v_sfm = ac.inc(SE.copy(sfm[:], pb[2][0:8, :]))
        set_res("pbn0_rd", (ac, v_sfm))
        return v_sfm

    def emit_ffn(s, v_sfm):
        TE.wait_ge(ac.h, v_sfm)
        v_stats = None
        v_bse = {}

        def emit_sbcast(e):
            bp = e % 2
            wait_res(VE, "selrow_rd")
            v_sel = ve.inc(VE.tensor_scalar(
                selrow[:], iota_c[0:8, 0:1].to_broadcast([8, 128]), float(e),
                None, op0=ALU.is_equal))
            TE.wait_ge(ve.h, v_sel)
            wait_res(TE, "pb4g_rd")
            v_sb = pe.inc(TE.matmul(pb[4][:, :], selrow[:],
                                    sfm[:], start=True, stop=True))
            set_res("selrow_rd", (pe, v_sb))
            SE.wait_ge(pe.h, v_sb)
            wait_res(SE, f"bcs_rd{bp}")
            v_bse[bp] = ac.inc(SE.copy(bc_s[bp][:], pb[4][:, :]))
            set_res("pb4g_rd", (ac, v_bse[bp]))

        emit_sbcast(0)
        for e in range(NEX):
            ep = e % 2
            hb = ep * 8
            v_hm = {}
            PRD = GE if POOL_H_PROD else VE
            sem_prd = gp if POOL_H_PROD else ve
            for hc in range(HCN):
                par = hc % 2
                ws = con_slot("wbig")
                q_wait(TE)
                wait_res(TE, f"pbh1{par}_rd")
                wait_res(TE, f"pbh3{par}_rd")
                ins = None
                for kc in range(C):
                    TE.matmul(pb[par][:, :], wbig[ws][:, kc * 128:(kc + 1) * 128],
                              actc(kc), start=(kc == 0), stop=(kc == C - 1))
                    ins = TE.matmul(pb[2 + par][:, :],
                                    wbig[ws][:, D + kc * 128:D + (kc + 1) * 128],
                                    actc(kc), start=(kc == 0), stop=(kc == C - 1))
                v_h = pe.inc(ins)
                free_slot("wbig", pe, v_h)
                q_pump(6)
                if e < 8:
                    VE.wait_ge(pe.h, v_h)
                    VE.wait_ge(ac.h, v_bse[ep])
                    wait_res(VE, f"fscr{par}0_rd")
                    v_h1s = ve.inc(VE.tensor_tensor(scrf(par, 0), pb[par][:, :],
                                                    bc_s[ep][:], op=ALU.mult))
                    set_res(f"pbh1{par}_rd", (ve, v_h1s))
                    SE.wait_ge(ve.h, v_h1s)
                    wait_res(SE, f"fscr{par}1_rd")
                    v_silu = ac.inc(SE.activation(scrf(par, 1), scrf(par, 0), AF.Silu))
                    set_res(f"fscr{par}0_rd", (ac, v_silu))
                    wait_res(VE, f"fscr{par}2_rd")
                    v_h3s = ve.inc(VE.tensor_tensor(scrf(par, 2), pb[2 + par][:, :],
                                                    bc_s[ep][:], op=ALU.mult))
                    set_res(f"pbh3{par}_rd", (ve, v_h3s))
                    v_h3s_last = v_h3s
                    PRD.wait_ge(ac.h, v_silu)
                    if POOL_H_PROD:
                        PRD.wait_ge(ve.h, v_h3s)
                    wait_res(PRD, f"h_rd{ep}")
                    v_m = sem_prd.inc(PRD.tensor_tensor(combc(hb + hc), scrf(par, 1),
                                                        scrf(par, 2), op=ALU.mult))
                    set_res(f"fscr{par}1_rd", (sem_prd, v_m))
                    set_res(f"fscr{par}2_rd", (sem_prd, v_m))
                else:
                    SE.wait_ge(pe.h, v_h)
                    wait_res(SE, f"fscr{par}1_rd")
                    v_silu = ac.inc(SE.activation(scrf(par, 1), pb[par][:, :], AF.Silu))
                    set_res(f"pbh1{par}_rd", (ac, v_silu))
                    VE.wait_ge(ac.h, v_silu)
                    wait_res(VE, f"h_rd{ep}")
                    v_m = ve.inc(VE.tensor_tensor(combc(hb + hc), scrf(par, 1),
                                                  pb[2 + par][:, :], op=ALU.mult))
                    set_res(f"pbh3{par}_rd", (ve, v_m))
                    set_res(f"fscr{par}1_rd", (ve, v_m))
                v_hm[hc] = (sem_prd if e < 8 else ve, v_m)
            if e < 8:
                set_res(f"bcs_rd{ep}", (ve, v_h3s_last))
            if e + 1 < 8:
                emit_sbcast(e + 1)
            v_w2_last = None
            for fc in range(C):
                par2 = fc % 2
                w2s = con_slot("w2b")
                q_wait(TE)
                wait_res(TE, f"pbw2{par2}_rd")
                ins = None
                for kc in range(HCN):
                    if fc == 0:
                        sem_h, v_h_ = v_hm[kc]
                        TE.wait_ge(sem_h.h, v_h_)
                    ins = TE.matmul(pb[5 + par2][:, :], w2b[w2s][:, kc * 128:(kc + 1) * 128],
                                    combc(hb + kc), start=(kc == 0), stop=(kc == HCN - 1))
                v_w2 = pe.inc(ins)
                free_slot("w2b", pe, v_w2)
                q_pump(6)
                if e == 0:
                    SE.wait_ge(pe.h, v_w2)
                    v_ev = ac.inc(SE.copy(s1c(fc), pb[5 + par2][:, :]))
                    set_res(f"pbw2{par2}_rd", (ac, v_ev))
                    set_res(f"s1w_{fc}", (ac, v_ev))
                elif POOL_W2_EVAC and fc % 2 == 1:
                    SE.wait_ge(pe.h, v_w2)
                    wait_res(SE, "bcb_rd")
                    v_cp = ac.inc(SE.copy(bc_b[:], pb[5 + par2][:, :]))
                    set_res(f"pbw2{par2}_rd", (ac, v_cp))
                    GE.wait_ge(ac.h, v_cp)
                    wait_res(GE, f"s1w_{fc}")
                    v_ev = gp.inc(GE.tensor_tensor(s1c(fc), bc_b[:], s1c(fc), op=ALU.add))
                    set_res("bcb_rd", (gp, v_ev))
                    set_res(f"s1w_{fc}", (gp, v_ev))
                else:
                    VE.wait_ge(pe.h, v_w2)
                    wait_res(VE, f"s1w_{fc}")
                    v_ev = ve.inc(VE.scalar_tensor_tensor(s1c(fc), pb[5 + par2][:, :], 0.0,
                                                          s1c(fc), op0=ALU.add, op1=ALU.add))
                    set_res(f"pbw2{par2}_rd", (ve, v_ev))
                    set_res(f"s1w_{fc}", (ve, v_ev))
                if e == NEX - 1 and fc >= 2:
                    ln_stat_chunk(fc - 2, s1c(fc - 2), ready=res[f"s1w_{fc - 2}"][0])
                v_w2_last = v_w2
            if e == NEX - 1:
                for fc in (C - 2, C - 1):
                    v = ln_stat_chunk(fc, s1c(fc), ready=res[f"s1w_{fc}"][0])
                    if v is not None:
                        v_stats = v
            set_res(f"h_rd{ep}", (pe, v_w2_last))
        return v_stats

    def dump(name, src, dst=None):
        if not debug:
            return
        for sm in (pe, ve, ac, gp):
            SY.wait_ge(sm.h, sm.n)
        qo.inc(SY.dma_start(out=dbg[name] if dst is None else dst, in_=src), 16)
        v = qo.n
        for e in (TE, VE, SE):
            e.wait_ge(qo.h, v)

    out_v = out_h.rearrange("(c p) t -> p c t", p=128)

    def store_sub(s):
        qo.inc(SY.dma_start(out=out_v[:, 0:8, s * ST:(s + 1) * ST],
                            in_=x_sb_v[:, 0:8, s * ST:(s + 1) * ST]), 16)
        qo.inc(SY.dma_start(out=out_v[:, 8:16, s * ST:(s + 1) * ST],
                            in_=x_sb_v[:, 8:16, s * ST:(s + 1) * ST]), 16)

    pending = None
    for s in range(NS):
        for h in range(H):
            q_register(*mk_head_unit(h))
        for fc in range(C):
            q_register(*mk_wo_unit(fc))
        g13 = 0
        g2 = 0
        for e in range(NEX):
            for hc in range(HCN):
                q_register(*mk_w13_unit(e, hc, g13))
                g13 += 1
            for fc in range(C):
                q_register(*mk_w2_unit(e, fc, g2))
                g2 += 1
        if s == 0:
            SE.wait_ge(qx.h, v_x)
        q_pump(6)

        if s == 0:
            dump("dbg_x0", x_sb[:])
            dump("dbg_wa", wa_sb[:])
            dump("dbg_waT", waT_sb[:])
            dump("dbg_sm", sm_sb[:])
            dump("dbg_s4", small4[:])
        v1 = emit_ln_stats(lambda c: xc(s, c))
        ap1 = emit_ln_finish(s, lambda c: xc(s, c), "pre", v1,
                             bct=(bc_s[0], bc_s[1]) if pending is not None else None)
        if s == 0:
            dump("dbg_xn", act[:])
        v_comb = emit_attn(s, q0_ready=ap1)
        if s == 0:
            dump("dbg_comb", comb[:])
        guard = None
        if pending is not None:
            sp, v_bcev_p = pending
            guard = emit_ln_applies_deferred(sp, v_bcev_p)
            SY.wait_ge(ve.h, guard[-1])
            store_sub(sp)
            pending = None
        v2 = emit_wo(s, v_comb, s1_guard=guard)
        if s == 0:
            dump("dbg_attn", s1[:])
        ap2 = emit_ln_finish(s, s1c, "post", v2)
        if s == 0 and debug:
            dump("dbg_x1", x_sb[:].rearrange("p (c t) -> p c t", c=C)[:, :, 0:ST],
                 dst=dbg["dbg_x1"].rearrange("p (c t) -> p c t", c=C))
        v3 = emit_ln_stats(lambda c: xc(s, c), src_ready=ap2)
        ap3 = emit_ln_finish(s, lambda c: xc(s, c), "pre", v3)
        if s == 0:
            dump("dbg_y", act[:])
        v_sfm = emit_gate(s, ready=ap3)
        if s == 0:
            dump("dbg_scores", scores[:])
            dump("dbg_sfm", sfm[:])
            dump("dbg_stm", stm[:])
            dump("dbg_scl", scl[:])
            dump("dbg_m01", m01[:])
        v4 = emit_ffn(s, v_sfm)
        if s == 0:
            dump("dbg_ffn", s1[:])
        if s + 1 < NS:
            pending = (s, emit_ln_finish(s, s1c, "post", v4, defer=True))
        else:
            emit_ln_finish(s, s1c, "post", v4)
            SY.wait_ge(ve.h, ve.n)
            SY.wait_ge(gp.h, gp.n)
            store_sub(s)

    SY.wait_ge(qo.h, qo.n)
    return nc


def _swz2(w):
    a, b = w.shape
    return np.ascontiguousarray(
        w.reshape(a // 128, 128, b // 128, 128).transpose(2, 1, 0, 3)
    ).reshape(b // 128, 128, a)


def kernel(**inputs):
    f32 = np.float32
    x = np.asarray(inputs["L0_attn_hidden_states"], dtype=f32)
    Wq = np.asarray(inputs["L0_attn_Wq"], dtype=f32)
    Wo = np.asarray(inputs["L0_attn_Wo"], dtype=f32)
    bw = np.asarray(inputs["L0_attn_bank_weights"], dtype=f32)
    mem = np.asarray(inputs["L0_attn_memories"], dtype=f32)
    mnorm = np.asarray(inputs["L0_attn_memory_norms"], dtype=f32)
    gw = np.asarray(inputs["L0_ffn_gate_w"], dtype=f32)
    w1 = np.asarray(inputs["L0_ffn_w1"], dtype=f32)
    w2 = np.asarray(inputs["L0_ffn_w2"], dtype=f32)
    w3 = np.asarray(inputs["L0_ffn_w3"], dtype=f32)
    ws1 = np.asarray(inputs["L0_ffn_ws1"], dtype=f32)
    ws2 = np.asarray(inputs["L0_ffn_ws2"], dtype=f32)
    ws3 = np.asarray(inputs["L0_ffn_ws3"], dtype=f32)

    xf = np.ascontiguousarray(x.reshape(NCORES * TCORE, D))
    xT_cores = [np.ascontiguousarray(xf[i * TCORE:(i + 1) * TCORE].T) for i in range(NCORES)]

    wq_host = _swz2(np.ascontiguousarray(Wq.T))
    wo_host = _swz2(np.ascontiguousarray(Wo.T))

    def swz_e(w_e):
        return _swz2(np.ascontiguousarray(w_e.T))

    w1_host = np.stack([swz_e(w1[e]) for e in range(8)] + [swz_e(ws1)])
    w3_host = np.stack([swz_e(w3[e]) for e in range(8)] + [swz_e(ws3)])
    w2_host = np.stack([swz_e(w2[e]) for e in range(8)] + [swz_e(ws2)])

    gw_host = np.ascontiguousarray(
        gw.T.reshape(16, 128, 8).transpose(1, 0, 2)).reshape(128, 128)
    mh_host = np.ascontiguousarray(np.concatenate(
        [mem.transpose(2, 1, 0, 3).reshape(128, H, NB * 128),
         mnorm.transpose(2, 1, 0)], axis=2))
    bw_host = np.ascontiguousarray(bw)
    nf_host = np.ascontiguousarray(mnorm.reshape(NB, H * 128))

    nc = build_program()
    shared = {
        "wq_h": wq_host, "wo_h": wo_host, "mh_h": mh_host,
        "w1_h": w1_host, "w3_h": w3_host, "w2_h": w2_host,
        "gw_h": gw_host, "bw_h": bw_host, "nf_h": nf_host,
    }
    in_maps = [dict(shared, xT=xT_cores[i]) for i in range(NCORES)]
    res = run_bass_kernel_spmd(nc, in_maps, core_ids=list(range(NCORES)))
    out = np.concatenate([np.asarray(r["out"]).T for r in res.results], axis=0)
    return np.ascontiguousarray(out.reshape(4, 2048, 2048)).astype(np.float32)
